# version 01
# baseline (speedup 1.0000x reference)
import numpy as np
import concourse.bass as bass
import concourse.mybir as mybir
from concourse.bass_utils import run_bass_kernel_spmd

F32 = mybir.dt.float32
BF16 = mybir.dt.bfloat16
F8 = mybir.dt.float8e4
AF = mybir.ActivationFunctionType
ALU = mybir.AluOpType
AX = mybir.AxisListType.X

T = 1024
NT = 8
D = 2048
ND = 16
HID = 8192
NH = 64
SQ = 2048
NQ = 16
KBIG = 12582912.0
INV_SQRT_D = 0.08838834764831845
EPS_LN = 1e-5
EPS_RMS = 1e-6
EPS_Q = 1e-8
QB = 127.0
FP8_MAX = 448.0

ENGINES = ("sync", "tensor", "vector", "scalar", "gpsimd")


class Build:
    def __init__(self):
        self.ops = {e: [] for e in ENGINES}
        self.ctr = {}

    def cur(self, s):
        return self.ctr.get(s, 0)

    def op(self, eng, fn, waits=(), inc=None, amt=1):
        self.ops[eng].append((fn, tuple(waits), inc, amt))
        if inc is not None:
            self.ctr[inc] = self.ctr.get(inc, 0) + amt
        return self.cur(inc) if inc else None

    def drain(self, eng):
        self.ops[eng].append(((lambda e: e.drain()), (), None, 0))


def build_nc(stop_after=99):
    nc = bass.Bass("TRN2", target_bir_lowering=False, debug=False)

    def din(name, shape, dt=F32):
        return nc.dram_tensor(name, shape, dt, kind="ExternalInput").ap()

    x_in = din("x_in", [T, D])
    wv_t = din("wv_t", [D, 128])
    bv_rep = din("bv_rep", [128, 128])
    nw_in = din("nw_in", [1, D])
    gm_in = din("gm_in", [1, D])
    w_up_t = din("w_up_t", [D, HID])
    w_dn_t = din("w_dn_t", [HID, D])
    qw = din("qw", [D, T])
    kw = din("kw", [D, T])
    vw = din("vw", [D, T])
    ow = din("ow", [T, D])
    w_fc_t = din("w_fc_t", [D, HID])
    w_pj_t = din("w_pj_t", [HID, D])
    st_up = din("st_up", [256, HID])
    st_dn = din("st_dn", [1024, D])
    st_fc = din("st_fc", [256, HID])
    st_pj = din("st_pj", [1024, D])
    eye_bf = din("eye_bf", [128, 128], BF16)
    eye_f8 = din("eye_f8", [128, 128], F8)
    eye_f32 = din("eye_f32", [128, 128])
    ones_row = din("ones_row", [1, 128])
    causal = din("causal", [128, 128])

    y_out = nc.dram_tensor("y_out", [T, D], F32, kind="ExternalOutput").ap()

    def idram(name, shape, dt=F32):
        return nc.dram_tensor(name, shape, dt).ap()

    x_cur = idram("x_cur", [T, D])
    cc_u_in = idram("cc_u_in", [T, D], BF16)
    cc_u_out = idram("cc_u_out", [SQ, D], BF16)
    cc_po_in = idram("cc_po_in", [SQ, D])
    cc_po_out = idram("cc_po_out", [T, D])
    ar_sum_in = idram("ar_sum_in", [1, 8])
    ar_sum_out = idram("ar_sum_out", [1, 8])
    ar_wmax_in = idram("ar_wmax_in", [1, 8])
    ar_wmax_out = idram("ar_wmax_out", [1, 8])
    ar_xmax_in = idram("ar_xmax_in", [1, 8])
    ar_xmax_out = idram("ar_xmax_out", [1, 8])
    ar_hmax_in = idram("ar_hmax_in", [1, 8])
    ar_hmax_out = idram("ar_hmax_out", [1, 8])
    h_spill = idram("h_spill", [NH, 128, T])
    attn1_sp = idram("attn1_sp", [4, 128, SQ], BF16)

    RG8 = [list(range(8))]
    RGP = [[0, 1], [2, 3], [4, 5], [6, 7]]

    B = Build()
    sems = {}
    for s in ("dC", "dSt", "dWv", "dX", "dB", "dU0", "dU1", "dD0", "dD1",
              "dW3", "dA1", "dG", "cc", "sT", "sV", "sS"):
        cm = nc.semaphore(s)
        sems[s] = cm.__enter__()

    OFS = [16512]

    def salloc(name, shape, dt, ofs=None):
        elt = {F32: 4, BF16: 2, F8: 1}[dt]
        nbytes = int(np.prod(shape[1:])) * elt
        if ofs is None:
            ofs = OFS[0]
            OFS[0] += (nbytes + 31) & ~31
        assert ofs + nbytes <= 229344, (name, ofs, nbytes)
        return nc.alloc_sbuf_tensor_at(name, shape, dt, offset=ofs)

    eyeb = salloc("eyeb", [128, 128], BF16)
    eye8 = salloc("eye8", [128, 128], F8)
    eyef = salloc("eyef", [128, 128], F32)
    onesr = salloc("onesr", [1, 128], F32)
    caus = salloc("caus", [128, 128], F32)
    bvr = salloc("bvr", [128, 128], F32)
    scal = salloc("scal", [128, 16], F32)
    stat = salloc("stat", [128, 64], F32)
    tiny = salloc("tiny", [128, 32], F32)
    t1px = salloc("t1px", [1, 64], F32)
    nw_rep = salloc("nw_rep", [128, D], F32)
    gm_rep = salloc("gm_rep", [128, D], F32)
    xa = salloc("xa", [128, D], F32)
    b1 = salloc("b1", [128, D], F32)
    b2 = salloc("b2", [128, D], F32)
    bb = salloc("bb", [128, D], BF16)
    csts = salloc("csts", [128, 2], F32)
    PHASE0 = OFS[0]

    pb = [nc.alloc_psum_tensor("pb%d" % i, [128, 512], F32) for i in range(8)]

    SUP, SDN, RSUP, RSDN, SFCW, SPJW, RSFCW, RSPJW = 0, 1, 2, 3, 4, 5, 6, 7
    RSX, SCFC, RSH, SCPJ = 8, 9, 10, 11

    def load(dst, src, sem, waits=()):
        c = B.op("sync", lambda e, d=dst, s=src: e.dma_start(out=d, in_=s),
                 waits=waits, inc=sem, amt=16)
        return (sem, c)

    def store(dst, src, waits=()):
        return B.op("gpsimd", lambda e, d=dst, s=src: e.dma_start(out=d, in_=s),
                    waits=waits, inc="dG", amt=16)

    def v_op(fn, waits=(), drain=True):
        r = B.op("vector", fn, waits=waits, inc="sV")
        if drain:
            B.drain("vector")
        return r

    def s_op(fn, waits=(), drain=True):
        r = B.op("scalar", fn, waits=waits, inc="sS")
        if drain:
            B.drain("scalar")
        return r

    def t_op(fn, waits=(), inc=None):
        return B.op("tensor", fn, waits=waits, inc=inc)

    def g_op(fn, waits=(), inc=None, amt=1):
        return B.op("gpsimd", fn, waits=waits, inc=inc, amt=amt)

    def wV():
        return ("sV", B.cur("sV"))

    def wS():
        return ("sS", B.cur("sS"))

    def wT():
        return ("sT", B.cur("sT"))

    def wL():
        return ("dL", B.cur("dL"))

    def wG():
        return ("dG", B.cur("dG"))


    def finish_early():
        for tt in range(NT):
            wx = load(xa[:, :], x_cur[tt * 128:(tt + 1) * 128, :], "dX",
                      waits=[wV(), wG()])
            v_op(lambda e: e.tensor_copy(b2[:, :], xa[:, :]), waits=[wx])
            store(y_out[tt * 128:(tt + 1) * 128, :], b2[:, :], waits=[wV()])
        gfin = B.cur("dG")
        g_op(lambda e, g=gfin: e.wait_ge(sems["dG"], g))

    def bn_mv(src, waits=(), width=D):
        ng = max(1, width // 512)
        cw = width // ng
        for g in range(ng):
            v_op(lambda e, g=g, cw=cw: e.bn_stats(
                tiny[:, g * 6:(g + 1) * 6], src[:, g * cw:(g + 1) * cw]),
                waits=waits if g == 0 else (), drain=(g == ng - 1))
        v_op(lambda e, ng=ng: e.bn_aggr(tiny[:, 24:26], tiny[:, 0:6 * ng]))

    def ln_stats(src, c0, eps, waits=(), width=D):
        bn_mv(src, waits=waits, width=width)
        v_op(lambda e, c=c0: e.tensor_copy(stat[:, c:c + 1], tiny[:, 24:25]))
        s_op(lambda e: e.activation(tiny[:, 26:27], tiny[:, 25:26], AF.Sqrt,
                                    bias=float(eps), scale=1.0), waits=[wV()])
        v_op(lambda e, c=c0: e.reciprocal(stat[:, c + 1:c + 2], tiny[:, 26:27]),
             waits=[wS()])

    def ln_apply(dst, src, c0, waits=()):
        v_op(lambda e, c=c0: e.tensor_scalar(dst, src, stat[:, c:c + 1],
                                             stat[:, c + 1:c + 2],
                                             ALU.subtract, ALU.mult), waits=waits)

    def transpose(dst, src, eye, ps, waits=()):
        t_op(lambda e: e.matmul(ps, src, eye, start=True, stop=True),
             waits=waits, inc="sT")
        s_op(lambda e: e.activation(dst, ps, AF.Copy, scale=1.0),
             waits=[wT()], drain=False)

    def bcast_row(dst, src_row, n, ps, waits=()):
        t_op(lambda e: e.matmul(ps[:, 0:n], onesr[0:1, 0:128], src_row,
                                start=True, stop=True),
             waits=list(waits) + [wS()], inc="sT")
        s_op(lambda e: e.activation(dst, ps[:, 0:n], AF.Copy, scale=1.0),
             waits=[wT()])

    def xpart(col, dcol, op, waits=()):
        t_op(lambda e: e.matmul(pb[7][0:1, 0:128], col, eyef[:, :],
                                start=True, stop=True), waits=waits, inc="sT")
        if op == "max":
            v_op(lambda e, d=dcol: e.reduce_max(t1px[0:1, d:d + 1],
                                                pb[7][0:1, 0:128], axis=AX),
                 waits=[wT()])
        else:
            v_op(lambda e, d=dcol: e.reduce_sum(t1px[0:1, d:d + 1],
                                                pb[7][0:1, 0:128], axis=AX),
                 waits=[wT()])

    v_op(lambda e: e.memset(csts[:, 0:1], EPS_LN), drain=False)
    v_op(lambda e: e.memset(csts[:, 1:2], EPS_RMS))
    nc.const_aps.register(F32, float(EPS_LN), csts[:, 0:1])
    nc.const_aps.register(F32, float(EPS_RMS), csts[:, 1:2])
    load(eyeb[:, :], eye_bf[:, :], "dC")
    load(eye8[:, :], eye_f8[:, :], "dC")
    load(eyef[:, :], eye_f32[:, :], "dC")
    load(onesr[:, :], ones_row[:, :], "dC")
    load(caus[:, :], causal[:, :], "dC")
    load(bvr[:, :], bv_rep[:, :], "dC")
    load(b1[0:1, 0:D], nw_in[:, :], "dC")
    load(b2[0:1, 0:D], gm_in[:, :], "dC")
    wconst = B.cur("dC")
    for g in range(4):
        bcast_row(nw_rep[:, g * 512:(g + 1) * 512],
                  b1[0:1, g * 512:(g + 1) * 512], 512, pb[g],
                  waits=[("dC", wconst)])
        bcast_row(gm_rep[:, g * 512:(g + 1) * 512],
                  b2[0:1, g * 512:(g + 1) * 512], 512, pb[g],
                  waits=[("dC", wconst)])

    stt = salloc("stt", [128, 2048], F32, PHASE0)
    v_op(lambda e: e.memset(tiny[:, 0:4], 0.0))
    for src, rows, cols, col, op in [
            (st_up, 256, HID, 0, "sum"), (st_dn, 1024, D, 1, "sum"),
            (st_fc, 256, HID, 2, "max"), (st_pj, 1024, D, 3, "max")]:
        for rr in range(rows // 128):
            for ccc in range(cols // 2048):
                wl = load(stt[:, :], src[rr * 128:(rr + 1) * 128,
                                         ccc * 2048:(ccc + 1) * 2048],
                          "dSt", waits=[wV()])
                if op == "sum":
                    v_op(lambda e: e.reduce_sum(tiny[:, 4:5], stt[:, :],
                                                axis=AX,
                                                apply_absolute_value=True),
                         waits=[wl])
                else:
                    v_op(lambda e: e.reduce_max(tiny[:, 4:5], stt[:, :],
                                                axis=AX,
                                                apply_absolute_value=True),
                         waits=[wl])
                aop = ALU.add if op == "sum" else ALU.max
                v_op(lambda e, c=col, o=aop: e.tensor_tensor(
                    tiny[:, c:c + 1], tiny[:, c:c + 1], tiny[:, 4:5], o))
    for col, op in [(0, "sum"), (1, "sum"), (2, "max"), (3, "max")]:
        xpart(tiny[:, col:col + 1], col, op, waits=[wV()])
        B.drain("vector")
    v_op(lambda e: e.memset(t1px[0:1, 16:32], 0.0))
    v_op(lambda e: e.tensor_copy(t1px[0:1, 16:18], t1px[0:1, 0:2]))
    v_op(lambda e: e.tensor_copy(t1px[0:1, 24:26], t1px[0:1, 2:4]))
    store(ar_sum_in[0:1, :], t1px[0:1, 16:24], waits=[wV()])
    st2 = store(ar_wmax_in[0:1, :], t1px[0:1, 24:32])
    g_op(lambda e: e.collective_compute(
        "AllReduce", ALU.add, replica_groups=RG8,
        ins=[ar_sum_in.tensor.ap().opt()],
        outs=[ar_sum_out.tensor.ap().opt()]),
        waits=[("dG", st2)], inc="cc")
    g_op(lambda e: e.collective_compute(
        "AllReduce", ALU.max, replica_groups=RG8,
        ins=[ar_wmax_in.tensor.ap().opt()],
        outs=[ar_wmax_out.tensor.ap().opt()]),
        inc="cc")
    cc_w = B.cur("cc")

    A = PHASE0
    wvf = salloc("wvf", [128, ND * 128], F32, A); A += ND * 128 * 4
    wvb = salloc("wvb", [128, ND * 128], BF16, A); A += ND * 128 * 2
    uT1 = salloc("uT1", [128, ND * 128], BF16, A); A += ND * 128 * 2
    vsb = salloc("vsb", [128, 128], F32, A); A += 512
    lnv = salloc("lnv", [128, 128], F32, A); A += 512
    for dc in range(ND):
        load(wvf[:, dc * 128:(dc + 1) * 128], wv_t[dc * 128:(dc + 1) * 128, :],
             "dWv", waits=[wV()])
    v_op(lambda e: e.tensor_copy(wvb[:, :], wvf[:, :]),
         waits=[("dWv", B.cur("dWv"))])
    wWv = B.cur("sV")
    for tt in range(NT):
        wx = load(xa[:, :], x_in[tt * 128:(tt + 1) * 128, :], "dX",
                  waits=[wG(), wV()])
        ln_stats(xa[:, :], 0, EPS_LN, waits=[wx])
        ln_apply(bb[:, :], xa[:, :], 0, waits=[wT()])
        wU = wV()
        for dc in range(ND):
            transpose(uT1[:, dc * 128:(dc + 1) * 128],
                      bb[:, dc * 128:(dc + 1) * 128], eyeb[:, :],
                      pb[0][:, 0:128], waits=[wU] if dc == 0 else [wS()])
        wtr = wS()
        for dc in range(ND):
            t_op(lambda e, dc=dc: e.matmul(
                pb[1][:, 0:128], uT1[:, dc * 128:(dc + 1) * 128],
                wvb[:, dc * 128:(dc + 1) * 128],
                start=(dc == 0), stop=(dc == ND - 1)),
                waits=[wtr, ("sV", wWv)] if dc == 0 else (),
                inc="sT" if dc == ND - 1 else None)
        v_op(lambda e: e.tensor_tensor(vsb[:, :], pb[1][:, 0:128], bvr[:, :],
                                       ALU.add), waits=[wT()])
        ln_stats(vsb[:, :], 2, EPS_LN, width=128)
        ln_apply(lnv[:, :], vsb[:, :], 2)
        for dc in range(ND):
            v_op(lambda e, dc=dc: e.tensor_tensor(
                xa[:, dc * 128:(dc + 1) * 128], xa[:, dc * 128:(dc + 1) * 128],
                lnv[:, :], ALU.add), drain=False)
        B.drain("vector")
        store(x_cur[tt * 128:(tt + 1) * 128, :], xa[:, :], waits=[wV()])

    if stop_after <= 1:
        finish_early()
    if stop_after <= 1:
        pass
    g_op(lambda e: e.wait_ge(sems["cc"], cc_w))
    store(t1px[0:1, 32:40], ar_sum_out[0:1, :])
    l2 = store(t1px[0:1, 40:48], ar_wmax_out[0:1, :])
    NUP = float(HID * D)
    v_op(lambda e: e.tensor_scalar(t1px[0:1, 0:2], t1px[0:1, 32:34],
                                   1.0 / NUP, EPS_Q, ALU.mult, ALU.max),
         waits=[("dG", l2)])
    v_op(lambda e: e.reciprocal(t1px[0:1, 2:4], t1px[0:1, 0:2]))
    v_op(lambda e: e.tensor_scalar(t1px[0:1, 4:6], t1px[0:1, 40:42],
                                   1.0 / FP8_MAX, 1e-6, ALU.mult, ALU.max))
    v_op(lambda e: e.reciprocal(t1px[0:1, 6:8], t1px[0:1, 4:6]))
    v_op(lambda e: e.tensor_scalar(t1px[0:1, 6:8], t1px[0:1, 6:8], 0.5, None,
                                   ALU.mult))
    bcast_row(scal[:, 0:8], t1px[0:1, 0:8], 8, pb[6], waits=[wV()])
    wScal0 = wS()

    if stop_after <= 1:
        return _emit(nc, B, sems)
    A = PHASE0
    xqT = salloc("xqT", [128, ND * T], BF16, A); A += ND * T * 2
    hhf = salloc("hhf", [128, NH * 512], BF16, A); A += NH * 512 * 2
    wupf = salloc("wupf", [128, 2 * ND * 128], F32, A); A += 2 * ND * 128 * 4
    wupq = salloc("wupq", [128, 2 * ND * 128], BF16, A); A += 2 * ND * 128 * 2
    wdnf = salloc("wdnf", [128, 2 * 512], F32, A); A += 2 * 512 * 4
    wdnq = salloc("wdnq", [128, 2 * 512], BF16, A); A += 2 * 512 * 2
    ost = salloc("ost", [128, 4 * D], BF16, A); A += 4 * D * 2

    for tt in range(NT):
        c0 = 8 + tt * 2
        wx = load(xa[:, :], x_cur[tt * 128:(tt + 1) * 128, :], "dX",
                  waits=[wV(), wG()])
        ln_stats(xa[:, :], c0, EPS_LN, waits=[wx])
        ln_apply(b1[:, :], xa[:, :], c0)
        bn_mv(b1[:, :])
        v_op(lambda e: e.tensor_tensor(tiny[:, 27:28], tiny[:, 24:25],
                                       tiny[:, 24:25], ALU.mult))
        v_op(lambda e: e.tensor_tensor(tiny[:, 28:29], tiny[:, 25:26],
                                       tiny[:, 27:28], ALU.add))
        s_op(lambda e: e.activation(tiny[:, 29:30], tiny[:, 28:29], AF.Sqrt,
                                    bias=EPS_RMS, scale=1.0), waits=[wV()])
        v_op(lambda e: e.reciprocal(tiny[:, 30:31], tiny[:, 29:30]),
             waits=[wS()])
        v_op(lambda e: e.tensor_scalar(b2[:, :], b1[:, :], tiny[:, 30:31],
                                       None, ALU.mult))
        v_op(lambda e: e.tensor_tensor(b2[:, :], b2[:, :], nw_rep[:, :],
                                       ALU.mult))
        v_op(lambda e: e.reduce_max(tiny[:, 0:1], b2[:, :], axis=AX,
                                    apply_absolute_value=True))
        v_op(lambda e: e.tensor_scalar(tiny[:, 1:2], tiny[:, 0:1], EPS_Q,
                                       1.0 / QB, ALU.max, ALU.mult))
        v_op(lambda e: e.reciprocal(tiny[:, 2:3], tiny[:, 1:2]))
        v_op(lambda e: e.tensor_scalar(b2[:, :], b2[:, :], tiny[:, 2:3], KBIG,
                                       ALU.mult, ALU.add))
        v_op(lambda e: e.tensor_scalar(bb[:, :], b2[:, :], KBIG, tiny[:, 1:2],
                                       ALU.subtract, ALU.mult), waits=[wT()])
        wQ = wV()
        for dc in range(ND):
            transpose(xqT[:, dc * T + tt * 128:dc * T + (tt + 1) * 128],
                      bb[:, dc * 128:(dc + 1) * 128], eyeb[:, :],
                      pb[0][:, 0:128], waits=[wQ] if dc == 0 else [wS()])
    wPrep = wS()

    up_done = {}
    dn_done = {}
    for half in range(2):
        tok0 = half * 512
        for hc in range(NH):
            slot = hc % 2
            so = slot * ND * 128
            idx = half * NH + hc
            wl_w = [wPrep] if idx < 2 else [("sT", up_done[idx - 2])]
            src3 = w_up_t[:, hc * 128:(hc + 1) * 128].rearrange(
                "(dc p) f -> p dc f", p=128)
            dst3 = wupf[:, so:so + ND * 128].rearrange(
                "p (dc f) -> p dc f", dc=ND)
            usem = "dU%d" % slot
            load(dst3[:, 0:8, :], src3[:, 0:8, :], usem, waits=wl_w)
            wl2 = load(dst3[:, 8:16, :], src3[:, 8:16, :], usem)
            v_op(lambda e, so=so: e.tensor_scalar(
                wupf[:, so:so + ND * 128], wupf[:, so:so + ND * 128],
                scal[:, RSUP:RSUP + 1], 1.49, ALU.mult, ALU.min),
                waits=[wl2, wScal0])
            v_op(lambda e, so=so: e.tensor_scalar(
                wupf[:, so:so + ND * 128], wupf[:, so:so + ND * 128],
                -1.49, KBIG, ALU.max, ALU.add))
            v_op(lambda e, so=so: e.tensor_scalar(
                wupq[:, so:so + ND * 128], wupf[:, so:so + ND * 128],
                KBIG, scal[:, SUP:SUP + 1], ALU.subtract, ALU.mult))
            wq = wV()
            for dc in range(ND):
                t_op(lambda e, so=so, dc=dc, s=slot: e.matmul(
                    pb[s][:, :], wupq[:, so + dc * 128:so + (dc + 1) * 128],
                    xqT[:, dc * T + tok0:dc * T + tok0 + 512],
                    start=(dc == 0), stop=(dc == ND - 1)),
                    waits=[wq, wS()] if dc == 0 else (),
                    inc="sT" if dc == ND - 1 else None)
            up_done[idx] = B.cur("sT")
            s_op(lambda e, hc=hc, s=slot: e.activation(
                hhf[:, hc * 512:(hc + 1) * 512], pb[s][:, :], AF.Silu,
                scale=1.0), waits=[wT()], drain=False)
        wUp = wS()
        for jg in range(4):
            for hc in range(NH):
                slot = hc % 2
                so = slot * 512
                idx = (half * 4 + jg) * NH + hc
                wl_w = [wUp] if (jg == 0 and hc < 2) else \
                    [("sT", dn_done[idx - 2])]
                wl = load(wdnf[:, so:so + 512],
                          w_dn_t[hc * 128:(hc + 1) * 128,
                                 jg * 512:(jg + 1) * 512], "dD%d" % slot,
                          waits=wl_w)
                v_op(lambda e, so=so: e.tensor_scalar(
                    wdnf[:, so:so + 512], wdnf[:, so:so + 512],
                    scal[:, RSDN:RSDN + 1], 1.49, ALU.mult, ALU.min),
                    waits=[wl, wScal0])
                v_op(lambda e, so=so: e.tensor_scalar(
                    wdnf[:, so:so + 512], wdnf[:, so:so + 512],
                    -1.49, KBIG, ALU.max, ALU.add))
                v_op(lambda e, so=so: e.tensor_scalar(
                    wdnq[:, so:so + 512], wdnf[:, so:so + 512],
                    KBIG, scal[:, SDN:SDN + 1], ALU.subtract, ALU.mult))
                wq = wV()
                for tl in range(4):
                    t_op(lambda e, so=so, tl=tl, hc=hc: e.matmul(
                        pb[2 + tl][:, :],
                        hhf[:, hc * 512 + tl * 128:hc * 512 + (tl + 1) * 128],
                        wdnq[:, so:so + 512],
                        start=(hc == 0), stop=(hc == NH - 1)),
                        waits=[wq, wS()] if (tl == 0 and hc == 0) else
                        ([wq] if tl == 0 else ()),
                        inc="sT" if tl == 3 else None)
                dn_done[idx] = B.cur("sT")
            wDn = wT()
            for tl in range(4):
                s_op(lambda e, tl=tl, jg=jg: e.activation(
                    ost[:, tl * D + jg * 512:tl * D + (jg + 1) * 512],
                    pb[2 + tl][:, :], AF.Copy, scale=1.0),
                    waits=[wDn, wV()], drain=False)
        wEv = wS()
        for tl in range(4):
            tt = half * 4 + tl
            c0 = 8 + tt * 2
            wx = load(xa[:, :], x_cur[tt * 128:(tt + 1) * 128, :], "dX",
                      waits=[wV(), wG()])
            v_op(lambda e, tl=tl: e.tensor_tensor(
                b1[:, :], ost[:, tl * D:(tl + 1) * D], gm_rep[:, :], ALU.mult),
                waits=[wEv, wx])
            ln_apply(b2[:, :], xa[:, :], c0)
            v_op(lambda e: e.tensor_tensor(b1[:, :], b1[:, :], b2[:, :],
                                           ALU.add))
            ln_stats(b1[:, :], 2, EPS_LN)
            ln_apply(b2[:, :], b1[:, :], 2)
            v_op(lambda e: e.tensor_tensor(xa[:, :], xa[:, :], b2[:, :],
                                           ALU.add))
            store(x_cur[tt * 128:(tt + 1) * 128, :], xa[:, :], waits=[wV()])

    if stop_after == 2:
        finish_early()
        return _emit(nc, B, sems)
    for tt in range(NT):
        wx = load(xa[:, :], x_cur[tt * 128:(tt + 1) * 128, :], "dX",
                  waits=[wV(), wG()])
        ln_stats(xa[:, :], 0, EPS_LN, waits=[wx])
        ln_apply(bb[:, :], xa[:, :], 0)
        store(cc_u_in[tt * 128:(tt + 1) * 128, :], bb[:, :], waits=[wV()])
    g_op(lambda e: e.collective_compute(
        "AllGather", ALU.bypass, replica_groups=RGP,
        ins=[cc_u_in.tensor.ap().opt()],
        outs=[cc_u_out.tensor.ap().opt()]),
        waits=[wG()], inc="cc")
    cc_ag = B.cur("cc")
    g_op(lambda e: e.wait_ge(sems["cc"], cc_ag))
    dmy = store(t1px[0:1, 48:56], ar_sum_out[0:1, :])

    A = PHASE0
    wst = salloc("wst", [128, ND * 128], F32, A); A += ND * 128 * 4
    qwb = salloc("qwb", [128, 4 * ND * 128], BF16, A); A += 4 * ND * 128 * 2
    kwb = salloc("kwb", [128, 4 * ND * 128], BF16, A); A += 4 * ND * 128 * 2
    vwb = salloc("vwb", [128, 4 * ND * 128], BF16, A); A += 4 * ND * 128 * 2
    uTg = salloc("uTg", [128, ND * 512], BF16, A); A += ND * 512 * 2
    qfm = salloc("qfm", [128, 4 * SQ], BF16, A); A += 4 * SQ * 2
    kfm = salloc("kfm", [128, 4 * SQ], BF16, A); A += 4 * SQ * 2
    vfm = salloc("vfm", [128, 4 * SQ], BF16, A); A += 4 * SQ * 2
    afm = salloc("afm", [128, 4 * SQ], BF16, A); A += 4 * SQ * 2
    vtm = salloc("vtm", [128, NQ * 128], BF16, A); A += NQ * 128 * 2
    psb = salloc("psb", [128, SQ], BF16, A); A += SQ * 2
    ptb = salloc("ptb", [128, 2 * 128], BF16, A); A += 2 * 128 * 2
    A_OPROJ = A

    p1_stored = 0
    for p in range(2):
        for wi, (wdram, wsb) in enumerate([(qw, qwb), (kw, kwb), (vw, vwb)]):
            for jc in range(4):
                col0 = p * 512 + jc * 128
                src3 = wdram[:, col0:col0 + 128].rearrange(
                    "(dc p) f -> p dc f", p=128)
                dst3 = wst[:, :].rearrange("p (dc f) -> p dc f", dc=ND)
                load(dst3[:, 0:8, :], src3[:, 0:8, :], "dW3", waits=[wV()])
                wl2 = load(dst3[:, 8:16, :], src3[:, 8:16, :], "dW3")
                v_op(lambda e, wsb=wsb, jc=jc: e.tensor_copy(
                    wsb[:, jc * ND * 128:(jc + 1) * ND * 128], wst[:, :]),
                    waits=[wl2])
        wWp = B.cur("sV")
        for tg in range(4):
            for i4 in range(4):
                r0 = tg * 512 + i4 * 128
                wl = load(bb[:, :], cc_u_out[r0:r0 + 128, :], "dB",
                          waits=[("dG", dmy), wS(), wV(), wT()])
                for dc in range(ND):
                    transpose(
                        uTg[:, dc * 512 + i4 * 128:dc * 512 + (i4 + 1) * 128],
                        bb[:, dc * 128:(dc + 1) * 128], eyeb[:, :],
                        pb[6][:, 0:128],
                        waits=[wl] if dc == 0 else [wS()])
            wtr = wS()
            for wi, (wsb, ofm) in enumerate([(qwb, qfm), (kwb, kfm),
                                             (vwb, vfm)]):
                for jc in range(4):
                    for dc in range(ND):
                        t_op(lambda e, wsb=wsb, jc=jc, dc=dc, wi=wi: e.matmul(
                            pb[wi][:, :],
                            wsb[:, jc * ND * 128 + dc * 128:
                                jc * ND * 128 + (dc + 1) * 128],
                            uTg[:, dc * 512:(dc + 1) * 512],
                            start=(dc == 0), stop=(dc == ND - 1)),
                            waits=[wtr, ("sV", wWp), wS()] if dc == 0 else (),
                            inc="sT" if dc == ND - 1 else None)
                    s_op(lambda e, ofm=ofm, jc=jc, tg=tg, wi=wi: e.activation(
                        ofm[:, jc * SQ + tg * 512:jc * SQ + (tg + 1) * 512],
                        pb[wi][:, :], AF.Copy, scale=1.0),
                        waits=[wT()], drain=False)
        wProj = wS()
        for hl in range(4):
            for kt in range(NQ):
                transpose(vtm[:, kt * 128:(kt + 1) * 128],
                          vfm[:, hl * SQ + kt * 128:hl * SQ + (kt + 1) * 128],
                          eyeb[:, :], pb[6][:, 0:128],
                          waits=[wProj] if kt == 0 else [wS()])
            wvt = wS()
            for qt in range(NQ):
                keys = (qt + 1) * 128
                nkg = (keys + 511) // 512
                for kg in range(nkg):
                    w = min(512, keys - kg * 512)
                    t_op(lambda e, hl=hl, qt=qt, kg=kg, w=w: e.matmul(
                        pb[kg][:, 0:w],
                        qfm[:, hl * SQ + qt * 128:hl * SQ + (qt + 1) * 128],
                        kfm[:, hl * SQ + kg * 512:hl * SQ + kg * 512 + w],
                        start=True, stop=True),
                        waits=[wvt, wV()] if kg == 0 else (),
                        inc="sT" if kg == nkg - 1 else None)
                wsc = wT()
                bd, od = qt // 4, (qt % 4) * 128
                v_op(lambda e, bd=bd, od=od: e.tensor_tensor(
                    pb[bd][:, od:od + 128], pb[bd][:, od:od + 128],
                    caus[:, :], ALU.add), waits=[wsc])
                for kg in range(nkg):
                    w = min(512, keys - kg * 512)
                    v_op(lambda e, kg=kg, w=w: e.reduce_max(
                        tiny[:, kg:kg + 1], pb[kg][:, 0:w], axis=AX),
                        drain=False)
                B.drain("vector")
                for kg in range(1, nkg):
                    v_op(lambda e, kg=kg: e.tensor_tensor(
                        tiny[:, 0:1], tiny[:, 0:1], tiny[:, kg:kg + 1],
                        ALU.max))
                v_op(lambda e: e.tensor_scalar(tiny[:, 4:5], tiny[:, 0:1],
                                               -INV_SQRT_D, None, ALU.mult))
                wmx = wV()
                for kg in range(nkg):
                    w = min(512, keys - kg * 512)
                    s_op(lambda e, kg=kg, w=w: e.activation(
                        psb[:, kg * 512:kg * 512 + w], pb[kg][:, 0:w], AF.Exp,
                        bias=tiny[:, 4:5], scale=INV_SQRT_D),
                        waits=[wmx, wT()] if kg == 0 else (), drain=False)
                B.drain("scalar")
                wexp = wS()
                v_op(lambda e, keys=keys: e.reduce_sum(
                    tiny[:, 5:6], psb[:, 0:keys], axis=AX), waits=[wexp])
                v_op(lambda e: e.reciprocal(tiny[:, 6:7], tiny[:, 5:6]))
                v_op(lambda e, keys=keys: e.tensor_scalar(
                    psb[:, 0:keys], psb[:, 0:keys], tiny[:, 6:7], None,
                    ALU.mult))
                wnorm = wV()
                for kt in range(qt + 1):
                    so = (kt % 2) * 128
                    t_op(lambda e, kt=kt: e.matmul(
                        pb[4][:, 0:128], psb[:, kt * 128:(kt + 1) * 128],
                        eyeb[:, :], start=True, stop=True),
                        waits=[wnorm] if kt == 0 else (), inc="sT")
                    s_op(lambda e, so=so: e.activation(
                        ptb[:, so:so + 128], pb[4][:, 0:128], AF.Copy,
                        scale=1.0), waits=[wT()], drain=False)
                    wpt = wS()
                    t_op(lambda e, kt=kt, so=so, qt=qt: e.matmul(
                        pb[5][:, 0:128], vtm[:, kt * 128:(kt + 1) * 128],
                        ptb[:, so:so + 128], start=(kt == 0), stop=(kt == qt)),
                        waits=[wpt], inc="sT" if kt == qt else None)
                ev_w = [wT()] + ([("dG", p1_stored)] if p == 1 else [])
                s_op(lambda e, hl=hl, qt=qt: e.activation(
                    afm[:, hl * SQ + qt * 128:hl * SQ + (qt + 1) * 128],
                    pb[5][:, 0:128], AF.Copy, scale=1.0),
                    waits=ev_w, drain=False)
        wAtt = wS()
        if p == 0:
            for hc in range(4):
                store(attn1_sp[hc, :, :], afm[:, hc * SQ:(hc + 1) * SQ],
                      waits=[wAtt])
            p1_stored = B.cur("dG")

    A = PHASE0
    owb = salloc("owb", [128, 8 * D], BF16, A); A += 8 * D * 2
    a1b = salloc("a1b", [128, 4 * 256], BF16, A); A += 4 * 256 * 2
    pos = salloc("pos", [128, 2 * D], F32, A); A += 2 * D * 4
    assert A <= PHASE0 + (ND * 128 * 4 + 3 * 4 * ND * 128 * 2 + ND * 512 * 2)
    for hc in range(8):
        wl = load(b1[:, :], ow[hc * 128:(hc + 1) * 128, :], "dB",
                  waits=[wV()])
        v_op(lambda e, hc=hc: e.tensor_copy(owb[:, hc * D:(hc + 1) * D],
                                            b1[:, :]), waits=[wl, wT()])
    wOw = B.cur("sV")
    for r in range(8):
        t0 = r * 256
        for hc in range(4):
            load(a1b[:, hc * 256:(hc + 1) * 256],
                 attn1_sp[hc, :, t0:t0 + 256], "dA1",
                 waits=[("dG", p1_stored), wT()])
        wlr = ("dA1", B.cur("dA1"))
        for tl in range(2):
            for hc in range(8):
                if hc < 4:
                    lhs = a1b[:, hc * 256 + tl * 128:hc * 256 + (tl + 1) * 128]
                else:
                    lhs = afm[:, (hc - 4) * SQ + t0 + tl * 128:
                              (hc - 4) * SQ + t0 + (tl + 1) * 128]
                for jg in range(4):
                    t_op(lambda e, lhs=lhs, jg=jg, hc=hc, tl=tl: e.matmul(
                        pb[tl * 4 + jg][:, :], lhs,
                        owb[:, hc * D + jg * 512:hc * D + (jg + 1) * 512],
                        start=(hc == 0), stop=(hc == 7)),
                        waits=[wlr, ("sV", wOw), wS()]
                        if (hc == 0 and jg == 0) else (),
                        inc="sT" if (hc == 7 and jg == 3) else None)
        wmm = wT()
        for tl in range(2):
            for jg in range(4):
                s_op(lambda e, tl=tl, jg=jg: e.activation(
                    pos[:, tl * D + jg * 512:tl * D + (jg + 1) * 512],
                    pb[tl * 4 + jg][:, :], AF.Copy, scale=1.0),
                    waits=[wmm, wG()], drain=False)
        B.drain("scalar")
        wev = wS()
        for tl in range(2):
            store(cc_po_in[t0 + tl * 128:t0 + (tl + 1) * 128, :],
                  pos[:, tl * D:(tl + 1) * D], waits=[wev])
    g_op(lambda e: e.collective_compute(
        "ReduceScatter", ALU.add, replica_groups=RGP,
        ins=[cc_po_in.tensor.ap().opt()],
        outs=[cc_po_out.tensor.ap().opt()]),
        waits=[wG()], inc="cc")
    cc_rs = B.cur("cc")
    g_op(lambda e: e.wait_ge(sems["cc"], cc_rs))
    dmy2 = store(t1px[0:1, 48:56], ar_sum_out[0:1, :])
    for tt in range(NT):
        wx = load(xa[:, :], x_cur[tt * 128:(tt + 1) * 128, :], "dX",
                  waits=[wV(), wG()])
        wp = load(b1[:, :], cc_po_out[tt * 128:(tt + 1) * 128, :], "dB",
                  waits=[("dG", dmy2), wV()])
        ln_stats(b1[:, :], 0, EPS_LN, waits=[wp])
        ln_apply(b2[:, :], b1[:, :], 0)
        v_op(lambda e: e.tensor_tensor(xa[:, :], xa[:, :], b2[:, :], ALU.add),
             waits=[wx])
        store(x_cur[tt * 128:(tt + 1) * 128, :], xa[:, :], waits=[wV()])

    if stop_after == 3:
        finish_early()
        return _emit(nc, B, sems)
    A = PHASE0
    x8T = salloc("x8T", [128, ND * T], F8, A); A += ND * T
    h8 = salloc("h8", [128, NH * T], F8, A); A += NH * T
    wf4 = salloc("wf4", [128, 2 * ND * 128], F32, A); A += 2 * ND * 128 * 4
    w84 = salloc("w84", [128, 2 * ND * 128], F8, A); A += 2 * ND * 128
    hsp = salloc("hsp", [128, 2 * T], F32, A); A += 2 * T * 4
    wpf = salloc("wpf", [128, 2 * 512], F32, A); A += 2 * 512 * 4
    wp8 = salloc("wp8", [128, 2 * 512], F8, A); A += 2 * 512
    ob4 = salloc("ob4", [128, NT * D], BF16, A); A += NT * D * 2
    x8m = salloc("x8m", [128, D], F8, A); A += D

    v_op(lambda e: e.memset(tiny[:, 3:4], 0.0))
    for tt in range(NT):
        c0 = 32 + tt * 2
        wx = load(xa[:, :], x_cur[tt * 128:(tt + 1) * 128, :], "dX",
                  waits=[wV(), wG()])
        ln_stats(xa[:, :], c0, EPS_LN, waits=[wx])
        ln_apply(b1[:, :], xa[:, :], c0)
        v_op(lambda e: e.reduce_max(tiny[:, 0:1], b1[:, :], axis=AX,
                                    apply_absolute_value=True))
        v_op(lambda e: e.tensor_tensor(tiny[:, 3:4], tiny[:, 3:4],
                                       tiny[:, 0:1], ALU.max))
    xpart(tiny[:, 3:4], 0, "max", waits=[wV()])
    v_op(lambda e: e.memset(t1px[0:1, 56:64], 0.0))
    v_op(lambda e: e.tensor_copy(t1px[0:1, 56:57], t1px[0:1, 0:1]))
    stx = store(ar_xmax_in[0:1, :], t1px[0:1, 56:64], waits=[wV()])
    g_op(lambda e: e.collective_compute(
        "AllReduce", ALU.max, replica_groups=RG8,
        ins=[ar_xmax_in.tensor.ap().opt()],
        outs=[ar_xmax_out.tensor.ap().opt()]),
        waits=[("dG", stx)], inc="cc")
    cc_x = B.cur("cc")
    g_op(lambda e: e.wait_ge(sems["cc"], cc_x))
    lx = store(t1px[0:1, 32:40], ar_xmax_out[0:1, :])
    v_op(lambda e: e.tensor_scalar(t1px[0:1, 8:9], t1px[0:1, 32:33],
                                   1.0 / FP8_MAX, 1e-6, ALU.mult, ALU.max),
         waits=[("dG", lx)])
    v_op(lambda e: e.reciprocal(t1px[0:1, 9:10], t1px[0:1, 8:9]))
    v_op(lambda e: e.tensor_scalar(t1px[0:1, 9:10], t1px[0:1, 9:10], 0.5,
                                   None, ALU.mult))
    v_op(lambda e: e.tensor_tensor(t1px[0:1, 10:11], t1px[0:1, 8:9],
                                   scal[0:1, SFCW:SFCW + 1], ALU.mult))
    v_op(lambda e: e.tensor_scalar(t1px[0:1, 10:11], t1px[0:1, 10:11], 4.0,
                                   None, ALU.mult))
    bcast_row(scal[:, 8:10], t1px[0:1, 9:11], 2, pb[6], waits=[wV()])
    for tt in range(NT):
        c0 = 32 + tt * 2
        wx = load(xa[:, :], x_cur[tt * 128:(tt + 1) * 128, :], "dX",
                  waits=[wV(), wG()])
        ln_apply(b1[:, :], xa[:, :], c0, waits=[wx, wS()])
        s_op(lambda e: e.activation(x8m[:, :], b1[:, :], AF.Copy,
                                    scale=scal[:, RSX:RSX + 1]),
             waits=[wV()])
        wq8 = wS()
        for dc in range(ND):
            transpose(x8T[:, dc * T + tt * 128:dc * T + (tt + 1) * 128],
                      x8m[:, dc * 128:(dc + 1) * 128], eye8[:, :],
                      pb[0][:, 0:128], waits=[wq8] if dc == 0 else [wS()])
    wQ8 = wS()
    v_op(lambda e: e.memset(tiny[:, 3:4], 0.0))
    fc_done = {}
    for hc in range(NH):
        slot = hc % 2
        so = slot * ND * 128
        wl_w = [wQ8] if hc < 2 else [("sT", fc_done[hc - 2])]
        src3 = w_fc_t[:, hc * 128:(hc + 1) * 128].rearrange(
            "(dc p) f -> p dc f", p=128)
        dst3 = wf4[:, so:so + ND * 128].rearrange("p (dc f) -> p dc f", dc=ND)
        usem = "dU%d" % slot
        load(dst3[:, 0:8, :], src3[:, 0:8, :], usem, waits=wl_w)
        wl2 = load(dst3[:, 8:16, :], src3[:, 8:16, :], usem)
        s_op(lambda e, so=so: e.activation(
            w84[:, so:so + ND * 128], wf4[:, so:so + ND * 128], AF.Copy,
            scale=scal[:, RSFCW:RSFCW + 1]), waits=[wl2])
        wq = wS()
        for tg in range(2):
            for dc in range(ND):
                t_op(lambda e, so=so, tg=tg, dc=dc: e.matmul(
                    pb[tg][:, :], w84[:, so + dc * 128:so + (dc + 1) * 128],
                    x8T[:, dc * T + tg * 512:dc * T + (tg + 1) * 512],
                    start=(dc == 0), stop=(dc == ND - 1)),
                    waits=[wq, wV()] if (tg == 0 and dc == 0) else (),
                    inc="sT" if dc == ND - 1 else None)
        fc_done[hc] = B.cur("sT")
        wmm = wT()
        so2 = slot * T
        s_op(lambda e, so2=so2: e.activation(
            hsp[:, so2:so2 + 512], pb[0][:, :], AF.Relu,
            scale=scal[:, SCFC:SCFC + 1]), waits=[wmm, wG()], drain=False)
        s_op(lambda e, so2=so2: e.activation(
            hsp[:, so2 + 512:so2 + 1024], pb[1][:, :], AF.Relu,
            scale=scal[:, SCFC:SCFC + 1]), waits=(), drain=True)
        v_op(lambda e, so2=so2: e.tensor_tensor(
            hsp[:, so2:so2 + T], hsp[:, so2:so2 + T], hsp[:, so2:so2 + T],
            ALU.mult), waits=[wS()])
        v_op(lambda e, so2=so2: e.reduce_max(tiny[:, 0:1],
                                             hsp[:, so2:so2 + T], axis=AX))
        v_op(lambda e: e.tensor_tensor(tiny[:, 3:4], tiny[:, 3:4],
                                       tiny[:, 0:1], ALU.max))
        store(h_spill[hc, :, :], hsp[:, so2:so2 + T], waits=[wV()])
    xpart(tiny[:, 3:4], 0, "max", waits=[wV()])
    v_op(lambda e: e.memset(t1px[0:1, 56:64], 0.0))
    v_op(lambda e: e.tensor_copy(t1px[0:1, 56:57], t1px[0:1, 0:1]))
    sth = store(ar_hmax_in[0:1, :], t1px[0:1, 56:64], waits=[wV()])
    g_op(lambda e: e.collective_compute(
        "AllReduce", ALU.max, replica_groups=RG8,
        ins=[ar_hmax_in.tensor.ap().opt()],
        outs=[ar_hmax_out.tensor.ap().opt()]),
        waits=[("dG", sth)], inc="cc")
    cc_h = B.cur("cc")
    g_op(lambda e: e.wait_ge(sems["cc"], cc_h))
    lh = store(t1px[0:1, 40:48], ar_hmax_out[0:1, :])
    v_op(lambda e: e.tensor_scalar(t1px[0:1, 8:9], t1px[0:1, 40:41],
                                   1.0 / FP8_MAX, 1e-6, ALU.mult, ALU.max),
         waits=[("dG", lh)])
    v_op(lambda e: e.reciprocal(t1px[0:1, 9:10], t1px[0:1, 8:9]))
    v_op(lambda e: e.tensor_scalar(t1px[0:1, 9:10], t1px[0:1, 9:10], 0.5,
                                   None, ALU.mult))
    v_op(lambda e: e.tensor_tensor(t1px[0:1, 10:11], t1px[0:1, 8:9],
                                   scal[0:1, SPJW:SPJW + 1], ALU.mult))
    v_op(lambda e: e.tensor_scalar(t1px[0:1, 10:11], t1px[0:1, 10:11], 4.0,
                                   None, ALU.mult))
    bcast_row(scal[:, 10:12], t1px[0:1, 9:11], 2, pb[6], waits=[wV()])
    wHs = wG()
    for hc in range(NH):
        wl = load(b1[:, 0:T], h_spill[hc, :, :], "dB", waits=[wHs, wS()])
        s_op(lambda e, hc=hc: e.activation(
            h8[:, hc * T:(hc + 1) * T], b1[:, 0:T], AF.Copy,
            scale=scal[:, RSH:RSH + 1]), waits=[wl], drain=False)
    wH8 = wS()
    pj_done = {}
    for jg in range(4):
        for hc in range(NH):
            slot = hc % 2
            so = slot * 512
            idx = jg * NH + hc
            wl_w = [wH8] if idx < 2 else [("sT", pj_done[idx - 2])]
            wl = load(wpf[:, so:so + 512],
                      w_pj_t[hc * 128:(hc + 1) * 128,
                             jg * 512:(jg + 1) * 512], "dD%d" % slot,
                      waits=wl_w)
            s_op(lambda e, so=so: e.activation(
                wp8[:, so:so + 512], wpf[:, so:so + 512], AF.Copy,
                scale=scal[:, RSPJW:RSPJW + 1]), waits=[wl],
                drain=False)
            wq = wS()
            for tt in range(NT):
                t_op(lambda e, so=so, tt=tt, hc=hc: e.matmul(
                    pb[tt][:, :],
                    h8[:, hc * T + tt * 128:hc * T + (tt + 1) * 128],
                    wp8[:, so:so + 512], start=(hc == 0), stop=(hc == NH - 1)),
                    waits=[wq] if tt == 0 else (),
                    inc="sT" if tt == 7 else None)
            pj_done[idx] = B.cur("sT")
        wmm = wT()
        for tt in range(NT):
            s_op(lambda e, tt=tt, jg=jg: e.activation(
                ob4[:, tt * D + jg * 512:tt * D + (jg + 1) * 512],
                pb[tt][:, :], AF.Copy, scale=scal[:, SCPJ:SCPJ + 1]),
                waits=[wmm], drain=False)
    B.drain("scalar")
    wEv4 = wS()
    for tt in range(NT):
        wx = load(xa[:, :], x_cur[tt * 128:(tt + 1) * 128, :], "dX",
                  waits=[wV(), wG()])
        ln_stats(ob4[:, tt * D:(tt + 1) * D], 0, EPS_LN,
                 waits=[wEv4, wx])
        ln_apply(b2[:, :], ob4[:, tt * D:(tt + 1) * D], 0)
        v_op(lambda e: e.tensor_tensor(xa[:, :], xa[:, :], b2[:, :], ALU.add))
        store(y_out[tt * 128:(tt + 1) * 128, :], xa[:, :], waits=[wV()])
    gfin = B.cur("dG")
    g_op(lambda e, g=gfin: e.wait_ge(sems["dG"], g))

    return _emit(nc, B, sems)


def _emit(nc, B, sems):
    with nc.Block() as block:
        def emit_for(name):
            def run(eng):
                hw = {}
                for fn, waits, inc, amt in B.ops[name]:
                    for s, tval in waits:
                        if tval > 0 and hw.get(s, -1) < tval:
                            eng.wait_ge(sems[s], tval)
                            hw[s] = tval
                    instr = fn(eng)
                    if inc is not None:
                        instr.then_inc(sems[inc], amt)
            return run

        block.sync(emit_for("sync"))
        block.tensor(emit_for("tensor"))
        block.vector(emit_for("vector"))
        block.scalar(emit_for("scalar"))
        block.gpsimd(emit_for("gpsimd"))

    return nc


_NC_CACHE = [None]


def kernel(**inputs):
    f32 = np.float32
    x = np.ascontiguousarray(np.asarray(inputs["L0_attn_x"],
                                        dtype=f32).reshape(8192, D))
    Wkv = np.asarray(inputs["L0_attn_Wkv"], dtype=f32)
    bkv = np.asarray(inputs["L0_attn_bkv"], dtype=f32)
    wv_t = np.ascontiguousarray(Wkv[128:256].T)
    bv_rep = np.ascontiguousarray(np.broadcast_to(bkv[128:256][None, :],
                                                  (128, 128)))
    nw = np.ascontiguousarray(np.asarray(inputs["L0_ffn_norm_weight"],
                                         dtype=f32)[None, :])
    gm = np.ascontiguousarray(np.asarray(inputs["L0_ffn_gamma"],
                                         dtype=f32)[None, :])
    w_up_t = np.ascontiguousarray(np.asarray(inputs["L0_ffn_weight_up"],
                                             dtype=f32).T)
    w_dn_t = np.ascontiguousarray(np.asarray(inputs["L0_ffn_weight_down"],
                                             dtype=f32).T)
    qwei = np.asarray(inputs["L1_attn_q_wei"], dtype=f32)
    kwei = np.asarray(inputs["L1_attn_k_wei"], dtype=f32)
    vwei = np.asarray(inputs["L1_attn_v_wei"], dtype=f32)
    owei = np.asarray(inputs["L1_attn_out_wei"], dtype=f32)
    w_fc_t = np.ascontiguousarray(np.asarray(inputs["L1_ffn_w_fc"],
                                             dtype=f32).T)
    w_pj_t = np.ascontiguousarray(np.asarray(inputs["L1_ffn_w_proj"],
                                             dtype=f32).T)

    import ml_dtypes
    eye_bf = np.eye(128, dtype=ml_dtypes.bfloat16)
    eye_f8 = np.eye(128, dtype=ml_dtypes.float8_e4m3fn)
    eye_f32 = np.eye(128, dtype=f32)
    ones_row = np.ones((1, 128), dtype=f32)
    causal = np.ascontiguousarray(
        np.triu(np.full((128, 128), -66504.0, dtype=f32), k=1))

    import os
    sa = int(os.environ.get("KSTOP", "99"))
    if _NC_CACHE[0] is None:
        _NC_CACHE[0] = build_nc(sa)
    nc = _NC_CACHE[0]

    in_maps = []
    for c in range(8):
        hsl = slice((c % 2) * 1024, (c % 2) * 1024 + 1024)
        in_maps.append({
            "x_in": np.ascontiguousarray(x[c * 1024:(c + 1) * 1024]),
            "wv_t": wv_t, "bv_rep": bv_rep, "nw_in": nw, "gm_in": gm,
            "w_up_t": w_up_t, "w_dn_t": w_dn_t,
            "qw": np.ascontiguousarray(qwei[:, hsl]),
            "kw": np.ascontiguousarray(kwei[:, hsl]),
            "vw": np.ascontiguousarray(vwei[:, hsl]),
            "ow": np.ascontiguousarray(owei[hsl, :]),
            "w_fc_t": w_fc_t, "w_pj_t": w_pj_t,
            "st_up": np.ascontiguousarray(
                w_up_t.reshape(-1)[c * 2097152:(c + 1) * 2097152]
                .reshape(256, HID)),
            "st_dn": np.ascontiguousarray(
                w_dn_t.reshape(-1)[c * 2097152:(c + 1) * 2097152]
                .reshape(1024, D)),
            "st_fc": np.ascontiguousarray(
                w_fc_t.reshape(-1)[c * 2097152:(c + 1) * 2097152]
                .reshape(256, HID)),
            "st_pj": np.ascontiguousarray(
                w_pj_t.reshape(-1)[c * 2097152:(c + 1) * 2097152]
                .reshape(1024, D)),
            "eye_bf": eye_bf, "eye_f8": eye_f8, "eye_f32": eye_f32,
            "ones_row": ones_row, "causal": causal,
        })
    res = run_bass_kernel_spmd(nc, in_maps, core_ids=list(range(8)))
    out = np.concatenate([np.asarray(r["y_out"]) for r in res.results], axis=0)
    return out.reshape(4, 2048, 2048)


# version 02
# speedup vs baseline: 1.0439x; 1.0439x over previous
import numpy as np
import concourse.bass as bass
import concourse.mybir as mybir
from concourse.bass_utils import run_bass_kernel_spmd

F32 = mybir.dt.float32
BF16 = mybir.dt.bfloat16
F8 = mybir.dt.float8e4
AF = mybir.ActivationFunctionType
ALU = mybir.AluOpType
AX = mybir.AxisListType.X

T = 1024
NT = 8
D = 2048
ND = 16
HID = 8192
NH = 64
SQ = 2048
NQ = 16
KBIG = 12582912.0
INV_SQRT_D = 0.08838834764831845
EPS_LN = 1e-5
EPS_RMS = 1e-6
EPS_Q = 1e-8
QB = 127.0
FP8_MAX = 448.0

ENGINES = ("sync", "tensor", "vector", "scalar", "gpsimd")


class Build:
    def __init__(self):
        self.ops = {e: [] for e in ENGINES}
        self.ctr = {}

    def cur(self, s):
        return self.ctr.get(s, 0)

    def op(self, eng, fn, waits=(), inc=None, amt=1):
        self.ops[eng].append((fn, tuple(waits), inc, amt))
        if inc is not None:
            self.ctr[inc] = self.ctr.get(inc, 0) + amt
        return self.cur(inc) if inc else None

    def drain(self, eng):
        self.ops[eng].append(((lambda e: e.drain()), (), None, 0))


def build_nc(stop_after=99):
    nc = bass.Bass("TRN2", target_bir_lowering=False, debug=False)

    def din(name, shape, dt=F32):
        return nc.dram_tensor(name, shape, dt, kind="ExternalInput").ap()

    x_in = din("x_in", [T, D])
    wv_t = din("wv_t", [D, 128])
    bv_rep = din("bv_rep", [128, 128])
    nw_in = din("nw_in", [1, D])
    gm_in = din("gm_in", [1, D])
    w_up_t = din("w_up_t", [D, HID])
    w_dn_t = din("w_dn_t", [HID, D])
    qw = din("qw", [D, T])
    kw = din("kw", [D, T])
    vw = din("vw", [D, T])
    ow = din("ow", [T, D])
    w_fc_t = din("w_fc_t", [D, HID])
    w_pj_t = din("w_pj_t", [HID, D])
    st_up = din("st_up", [256, HID])
    st_dn = din("st_dn", [1024, D])
    st_fc = din("st_fc", [256, HID])
    st_pj = din("st_pj", [1024, D])
    eye_bf = din("eye_bf", [128, 128], BF16)
    eye_f8 = din("eye_f8", [128, 128], F8)
    eye_f32 = din("eye_f32", [128, 128])
    ones_row = din("ones_row", [1, 128])
    causal = din("causal", [128, 128])

    y_out = nc.dram_tensor("y_out", [T, D], F32, kind="ExternalOutput").ap()

    def idram(name, shape, dt=F32):
        return nc.dram_tensor(name, shape, dt).ap()

    x_cur = idram("x_cur", [T, D])
    cc_u_in = idram("cc_u_in", [T, D], BF16)
    cc_u_out = idram("cc_u_out", [SQ, D], BF16)
    cc_po_in = idram("cc_po_in", [SQ, D])
    cc_po_out = idram("cc_po_out", [T, D])
    ar_sum_in = idram("ar_sum_in", [1, 8])
    ar_sum_out = idram("ar_sum_out", [1, 8])
    ar_wmax_in = idram("ar_wmax_in", [1, 8])
    ar_wmax_out = idram("ar_wmax_out", [1, 8])
    ar_xmax_in = idram("ar_xmax_in", [1, 8])
    ar_xmax_out = idram("ar_xmax_out", [1, 8])
    ar_hmax_in = idram("ar_hmax_in", [1, 8])
    ar_hmax_out = idram("ar_hmax_out", [1, 8])
    h_spill = idram("h_spill", [NH, 128, T])
    attn1_sp = idram("attn1_sp", [4, 128, SQ], BF16)

    RG8 = [list(range(8))]
    RGP = [[0, 1], [2, 3], [4, 5], [6, 7]]

    B = Build()
    sems = {}
    for s in ("dC", "dSt", "dWv", "dX", "dB", "dU0", "dU1", "dD0", "dD1",
              "dW3", "dA1", "dG", "cc", "sT", "sV", "sS"):
        cm = nc.semaphore(s)
        sems[s] = cm.__enter__()

    OFS = [16512]

    def salloc(name, shape, dt, ofs=None):
        elt = {F32: 4, BF16: 2, F8: 1}[dt]
        nbytes = int(np.prod(shape[1:])) * elt
        if ofs is None:
            ofs = OFS[0]
            OFS[0] += (nbytes + 31) & ~31
        assert ofs + nbytes <= 229344, (name, ofs, nbytes)
        return nc.alloc_sbuf_tensor_at(name, shape, dt, offset=ofs)

    eyeb = salloc("eyeb", [128, 128], BF16)
    eye8 = salloc("eye8", [128, 128], F8)
    eyef = salloc("eyef", [128, 128], F32)
    onesr = salloc("onesr", [1, 128], F32)
    caus = salloc("caus", [128, 128], F32)
    bvr = salloc("bvr", [128, 128], F32)
    scal = salloc("scal", [128, 16], F32)
    stat = salloc("stat", [128, 64], F32)
    tiny = salloc("tiny", [128, 32], F32)
    t1px = salloc("t1px", [1, 64], F32)
    nw_rep = salloc("nw_rep", [128, D], F32)
    gm_rep = salloc("gm_rep", [128, D], F32)
    xa = salloc("xa", [128, D], F32)
    b1 = salloc("b1", [128, D], F32)
    b2 = salloc("b2", [128, D], F32)
    bb = salloc("bb", [128, D], BF16)
    csts = salloc("csts", [128, 2], F32)
    PHASE0 = OFS[0]

    pb = [nc.alloc_psum_tensor("pb%d" % i, [128, 512], F32) for i in range(8)]

    SUP, SDN, RSUP, RSDN, SFCW, SPJW, RSFCW, RSPJW = 0, 1, 2, 3, 4, 5, 6, 7
    RSX, SCFC, RSH, SCPJ = 8, 9, 10, 11

    def load(dst, src, sem, waits=()):
        c = B.op("sync", lambda e, d=dst, s=src: e.dma_start(out=d, in_=s),
                 waits=waits, inc=sem, amt=16)
        return (sem, c)

    def store(dst, src, waits=()):
        return B.op("gpsimd", lambda e, d=dst, s=src: e.dma_start(out=d, in_=s),
                    waits=waits, inc="dG", amt=16)

    def v_op(fn, waits=(), drain=True):
        r = B.op("vector", fn, waits=waits, inc="sV")
        if drain:
            B.drain("vector")
        return r

    def s_op(fn, waits=(), drain=True):
        r = B.op("scalar", fn, waits=waits, inc="sS")
        if drain:
            B.drain("scalar")
        return r

    def t_op(fn, waits=(), inc=None):
        return B.op("tensor", fn, waits=waits, inc=inc)

    def g_op(fn, waits=(), inc=None, amt=1):
        return B.op("gpsimd", fn, waits=waits, inc=inc, amt=amt)

    def wV():
        return ("sV", B.cur("sV"))

    def wS():
        return ("sS", B.cur("sS"))

    def wT():
        return ("sT", B.cur("sT"))

    def wL():
        return ("dL", B.cur("dL"))

    def wG():
        return ("dG", B.cur("dG"))


    def finish_early():
        for tt in range(NT):
            wx = load(xa[:, :], x_cur[tt * 128:(tt + 1) * 128, :], "dX",
                      waits=[wV(), wG()])
            v_op(lambda e: e.tensor_copy(b2[:, :], xa[:, :]), waits=[wx])
            store(y_out[tt * 128:(tt + 1) * 128, :], b2[:, :], waits=[wV()])
        gfin = B.cur("dG")
        g_op(lambda e, g=gfin: e.wait_ge(sems["dG"], g))

    def bn_mv(src, waits=(), width=D):
        ng = max(1, width // 512)
        cw = width // ng
        for g in range(ng):
            v_op(lambda e, g=g, cw=cw: e.bn_stats(
                tiny[:, g * 6:(g + 1) * 6], src[:, g * cw:(g + 1) * cw]),
                waits=waits if g == 0 else (), drain=(g == ng - 1))
        v_op(lambda e, ng=ng: e.bn_aggr(tiny[:, 24:26], tiny[:, 0:6 * ng]))

    def ln_stats(src, c0, eps, waits=(), width=D):
        bn_mv(src, waits=waits, width=width)
        v_op(lambda e, c=c0: e.tensor_copy(stat[:, c:c + 1], tiny[:, 24:25]))
        s_op(lambda e: e.activation(tiny[:, 26:27], tiny[:, 25:26], AF.Sqrt,
                                    bias=float(eps), scale=1.0), waits=[wV()])
        v_op(lambda e, c=c0: e.reciprocal(stat[:, c + 1:c + 2], tiny[:, 26:27]),
             waits=[wS()])

    def ln_apply(dst, src, c0, waits=()):
        v_op(lambda e, c=c0: e.tensor_scalar(dst, src, stat[:, c:c + 1],
                                             stat[:, c + 1:c + 2],
                                             ALU.subtract, ALU.mult), waits=waits)

    def transpose(dst, src, eye, ps, waits=()):
        t_op(lambda e: e.matmul(ps, src, eye, start=True, stop=True),
             waits=waits, inc="sT")
        s_op(lambda e: e.activation(dst, ps, AF.Copy, scale=1.0),
             waits=[wT()], drain=False)

    def bcast_row(dst, src_row, n, ps, waits=()):
        t_op(lambda e: e.matmul(ps[:, 0:n], onesr[0:1, 0:128], src_row,
                                start=True, stop=True),
             waits=list(waits) + [wS()], inc="sT")
        s_op(lambda e: e.activation(dst, ps[:, 0:n], AF.Copy, scale=1.0),
             waits=[wT()])

    def xpart(col, dcol, op, waits=()):
        t_op(lambda e: e.matmul(pb[7][0:1, 0:128], col, eyef[:, :],
                                start=True, stop=True), waits=waits, inc="sT")
        if op == "max":
            v_op(lambda e, d=dcol: e.reduce_max(t1px[0:1, d:d + 1],
                                                pb[7][0:1, 0:128], axis=AX),
                 waits=[wT()])
        else:
            v_op(lambda e, d=dcol: e.reduce_sum(t1px[0:1, d:d + 1],
                                                pb[7][0:1, 0:128], axis=AX),
                 waits=[wT()])

    v_op(lambda e: e.memset(csts[:, 0:1], EPS_LN), drain=False)
    v_op(lambda e: e.memset(csts[:, 1:2], EPS_RMS))
    nc.const_aps.register(F32, float(EPS_LN), csts[:, 0:1])
    nc.const_aps.register(F32, float(EPS_RMS), csts[:, 1:2])
    load(eyeb[:, :], eye_bf[:, :], "dC")
    load(eye8[:, :], eye_f8[:, :], "dC")
    load(eyef[:, :], eye_f32[:, :], "dC")
    load(onesr[:, :], ones_row[:, :], "dC")
    load(caus[:, :], causal[:, :], "dC")
    load(bvr[:, :], bv_rep[:, :], "dC")
    load(b1[0:1, 0:D], nw_in[:, :], "dC")
    load(b2[0:1, 0:D], gm_in[:, :], "dC")
    wconst = B.cur("dC")
    for g in range(4):
        bcast_row(nw_rep[:, g * 512:(g + 1) * 512],
                  b1[0:1, g * 512:(g + 1) * 512], 512, pb[g],
                  waits=[("dC", wconst)])
        bcast_row(gm_rep[:, g * 512:(g + 1) * 512],
                  b2[0:1, g * 512:(g + 1) * 512], 512, pb[g],
                  waits=[("dC", wconst)])

    stt = salloc("stt", [128, 2048], F32, PHASE0)
    v_op(lambda e: e.memset(tiny[:, 0:4], 0.0))
    for src, rows, cols, col, op in [
            (st_up, 256, HID, 0, "sum"), (st_dn, 1024, D, 1, "sum"),
            (st_fc, 256, HID, 2, "max"), (st_pj, 1024, D, 3, "max")]:
        for rr in range(rows // 128):
            for ccc in range(cols // 2048):
                wl = load(stt[:, :], src[rr * 128:(rr + 1) * 128,
                                         ccc * 2048:(ccc + 1) * 2048],
                          "dSt", waits=[wV()])
                if op == "sum":
                    v_op(lambda e: e.reduce_sum(tiny[:, 4:5], stt[:, :],
                                                axis=AX,
                                                apply_absolute_value=True),
                         waits=[wl])
                else:
                    v_op(lambda e: e.reduce_max(tiny[:, 4:5], stt[:, :],
                                                axis=AX,
                                                apply_absolute_value=True),
                         waits=[wl])
                aop = ALU.add if op == "sum" else ALU.max
                v_op(lambda e, c=col, o=aop: e.tensor_tensor(
                    tiny[:, c:c + 1], tiny[:, c:c + 1], tiny[:, 4:5], o))
    for col, op in [(0, "sum"), (1, "sum"), (2, "max"), (3, "max")]:
        xpart(tiny[:, col:col + 1], col, op, waits=[wV()])
        B.drain("vector")
    v_op(lambda e: e.memset(t1px[0:1, 16:32], 0.0))
    v_op(lambda e: e.tensor_copy(t1px[0:1, 16:18], t1px[0:1, 0:2]))
    v_op(lambda e: e.tensor_copy(t1px[0:1, 24:26], t1px[0:1, 2:4]))
    store(ar_sum_in[0:1, :], t1px[0:1, 16:24], waits=[wV()])
    st2 = store(ar_wmax_in[0:1, :], t1px[0:1, 24:32])
    g_op(lambda e: e.collective_compute(
        "AllReduce", ALU.add, replica_groups=RG8,
        ins=[ar_sum_in.tensor.ap().opt()],
        outs=[ar_sum_out.tensor.ap().opt()]),
        waits=[("dG", st2)], inc="cc")
    g_op(lambda e: e.collective_compute(
        "AllReduce", ALU.max, replica_groups=RG8,
        ins=[ar_wmax_in.tensor.ap().opt()],
        outs=[ar_wmax_out.tensor.ap().opt()]),
        inc="cc")
    cc_w = B.cur("cc")

    A = PHASE0
    wvf = salloc("wvf", [128, ND * 128], F32, A); A += ND * 128 * 4
    wvb = salloc("wvb", [128, ND * 128], BF16, A); A += ND * 128 * 2
    uT1 = salloc("uT1", [128, ND * 128], BF16, A); A += ND * 128 * 2
    vsb = salloc("vsb", [128, 128], F32, A); A += 512
    lnv = salloc("lnv", [128, 128], F32, A); A += 512
    for dc in range(ND):
        load(wvf[:, dc * 128:(dc + 1) * 128], wv_t[dc * 128:(dc + 1) * 128, :],
             "dWv", waits=[wV()])
    v_op(lambda e: e.tensor_copy(wvb[:, :], wvf[:, :]),
         waits=[("dWv", B.cur("dWv"))])
    wWv = B.cur("sV")
    for tt in range(NT):
        wx = load(xa[:, :], x_in[tt * 128:(tt + 1) * 128, :], "dX",
                  waits=[wG(), wV()])
        ln_stats(xa[:, :], 0, EPS_LN, waits=[wx])
        ln_apply(bb[:, :], xa[:, :], 0, waits=[wT()])
        wU = wV()
        for dc in range(ND):
            transpose(uT1[:, dc * 128:(dc + 1) * 128],
                      bb[:, dc * 128:(dc + 1) * 128], eyeb[:, :],
                      pb[0][:, 0:128], waits=[wU] if dc == 0 else [wS()])
        wtr = wS()
        for dc in range(ND):
            t_op(lambda e, dc=dc: e.matmul(
                pb[1][:, 0:128], uT1[:, dc * 128:(dc + 1) * 128],
                wvb[:, dc * 128:(dc + 1) * 128],
                start=(dc == 0), stop=(dc == ND - 1)),
                waits=[wtr, ("sV", wWv)] if dc == 0 else (),
                inc="sT" if dc == ND - 1 else None)
        v_op(lambda e: e.tensor_tensor(vsb[:, :], pb[1][:, 0:128], bvr[:, :],
                                       ALU.add), waits=[wT()])
        ln_stats(vsb[:, :], 2, EPS_LN, width=128)
        ln_apply(lnv[:, :], vsb[:, :], 2)
        for dc in range(ND):
            v_op(lambda e, dc=dc: e.tensor_tensor(
                xa[:, dc * 128:(dc + 1) * 128], xa[:, dc * 128:(dc + 1) * 128],
                lnv[:, :], ALU.add), drain=False)
        B.drain("vector")
        store(x_cur[tt * 128:(tt + 1) * 128, :], xa[:, :], waits=[wV()])

    if stop_after <= 1:
        finish_early()
    if stop_after <= 1:
        pass
    g_op(lambda e: e.wait_ge(sems["cc"], cc_w))
    store(t1px[0:1, 32:40], ar_sum_out[0:1, :])
    l2 = store(t1px[0:1, 40:48], ar_wmax_out[0:1, :])
    NUP = float(HID * D)
    v_op(lambda e: e.tensor_scalar(t1px[0:1, 0:2], t1px[0:1, 32:34],
                                   1.0 / NUP, EPS_Q, ALU.mult, ALU.max),
         waits=[("dG", l2)])
    v_op(lambda e: e.reciprocal(t1px[0:1, 2:4], t1px[0:1, 0:2]))
    v_op(lambda e: e.tensor_scalar(t1px[0:1, 4:6], t1px[0:1, 40:42],
                                   1.0 / FP8_MAX, 1e-6, ALU.mult, ALU.max))
    v_op(lambda e: e.reciprocal(t1px[0:1, 6:8], t1px[0:1, 4:6]))
    v_op(lambda e: e.tensor_scalar(t1px[0:1, 6:8], t1px[0:1, 6:8], 0.5, None,
                                   ALU.mult))
    bcast_row(scal[:, 0:8], t1px[0:1, 0:8], 8, pb[6], waits=[wV()])
    wScal0 = wS()

    if stop_after <= 1:
        return _emit(nc, B, sems)
    A = PHASE0
    xqT = salloc("xqT", [128, ND * T], BF16, A); A += ND * T * 2
    hhf = salloc("hhf", [128, NH * 512], BF16, A); A += NH * 512 * 2
    wupf = salloc("wupf", [128, 2 * ND * 128], F32, A); A += 2 * ND * 128 * 4
    wupq = salloc("wupq", [128, 2 * ND * 128], BF16, A); A += 2 * ND * 128 * 2
    wdnf = salloc("wdnf", [128, 2 * 512], F32, A); A += 2 * 512 * 4
    wdnq = salloc("wdnq", [128, 2 * 512], BF16, A); A += 2 * 512 * 2
    ost = salloc("ost", [128, 4 * D], BF16, A); A += 4 * D * 2

    for tt in range(NT):
        c0 = 8 + tt * 2
        wx = load(xa[:, :], x_cur[tt * 128:(tt + 1) * 128, :], "dX",
                  waits=[wV(), wG()])
        ln_stats(xa[:, :], c0, EPS_LN, waits=[wx])
        ln_apply(b1[:, :], xa[:, :], c0)
        bn_mv(b1[:, :])
        v_op(lambda e: e.tensor_tensor(tiny[:, 27:28], tiny[:, 24:25],
                                       tiny[:, 24:25], ALU.mult))
        v_op(lambda e: e.tensor_tensor(tiny[:, 28:29], tiny[:, 25:26],
                                       tiny[:, 27:28], ALU.add))
        s_op(lambda e: e.activation(tiny[:, 29:30], tiny[:, 28:29], AF.Sqrt,
                                    bias=EPS_RMS, scale=1.0), waits=[wV()])
        v_op(lambda e: e.reciprocal(tiny[:, 30:31], tiny[:, 29:30]),
             waits=[wS()])
        v_op(lambda e: e.tensor_scalar(b2[:, :], b1[:, :], tiny[:, 30:31],
                                       None, ALU.mult))
        v_op(lambda e: e.tensor_tensor(b2[:, :], b2[:, :], nw_rep[:, :],
                                       ALU.mult))
        v_op(lambda e: e.reduce_max(tiny[:, 0:1], b2[:, :], axis=AX,
                                    apply_absolute_value=True))
        v_op(lambda e: e.tensor_scalar(tiny[:, 1:2], tiny[:, 0:1], EPS_Q,
                                       1.0 / QB, ALU.max, ALU.mult))
        v_op(lambda e: e.reciprocal(tiny[:, 2:3], tiny[:, 1:2]))
        v_op(lambda e: e.tensor_scalar(b2[:, :], b2[:, :], tiny[:, 2:3], KBIG,
                                       ALU.mult, ALU.add))
        v_op(lambda e: e.tensor_scalar(bb[:, :], b2[:, :], KBIG, tiny[:, 1:2],
                                       ALU.subtract, ALU.mult), waits=[wT()])
        wQ = wV()
        for dc in range(ND):
            transpose(xqT[:, dc * T + tt * 128:dc * T + (tt + 1) * 128],
                      bb[:, dc * 128:(dc + 1) * 128], eyeb[:, :],
                      pb[0][:, 0:128], waits=[wQ] if dc == 0 else [wS()])
    wPrep = wS()

    up_done = {}
    dn_done = {}
    for half in range(2):
        tok0 = half * 512
        for hc in range(NH):
            slot = hc % 2
            so = slot * ND * 128
            idx = half * NH + hc
            wl_w = [wPrep] if idx < 2 else [("sT", up_done[idx - 2])]
            src3 = w_up_t[:, hc * 128:(hc + 1) * 128].rearrange(
                "(dc p) f -> p dc f", p=128)
            dst3 = wupf[:, so:so + ND * 128].rearrange(
                "p (dc f) -> p dc f", dc=ND)
            usem = "dU%d" % slot
            load(dst3[:, 0:8, :], src3[:, 0:8, :], usem, waits=wl_w)
            wl2 = load(dst3[:, 8:16, :], src3[:, 8:16, :], usem)
            v_op(lambda e, so=so: e.tensor_scalar(
                wupf[:, so:so + ND * 128], wupf[:, so:so + ND * 128],
                scal[:, RSUP:RSUP + 1], 1.49, ALU.mult, ALU.min),
                waits=[wl2, wScal0])
            v_op(lambda e, so=so: e.tensor_scalar(
                wupf[:, so:so + ND * 128], wupf[:, so:so + ND * 128],
                -1.49, KBIG, ALU.max, ALU.add))
            v_op(lambda e, so=so: e.tensor_scalar(
                wupq[:, so:so + ND * 128], wupf[:, so:so + ND * 128],
                KBIG, scal[:, SUP:SUP + 1], ALU.subtract, ALU.mult))
            wq = wV()
            for dc in range(ND):
                t_op(lambda e, so=so, dc=dc, s=slot: e.matmul(
                    pb[s][:, :], wupq[:, so + dc * 128:so + (dc + 1) * 128],
                    xqT[:, dc * T + tok0:dc * T + tok0 + 512],
                    start=(dc == 0), stop=(dc == ND - 1)),
                    waits=[wq, wS()] if dc == 0 else (),
                    inc="sT" if dc == ND - 1 else None)
            up_done[idx] = B.cur("sT")
            s_op(lambda e, hc=hc, s=slot: e.activation(
                hhf[:, hc * 512:(hc + 1) * 512], pb[s][:, :], AF.Silu,
                scale=1.0), waits=[wT()], drain=False)
        wUp = wS()
        for jg in range(4):
            for hc in range(NH):
                slot = hc % 2
                so = slot * 512
                idx = (half * 4 + jg) * NH + hc
                wl_w = [wUp] if (jg == 0 and hc < 2) else \
                    [("sT", dn_done[idx - 2])]
                wl = load(wdnf[:, so:so + 512],
                          w_dn_t[hc * 128:(hc + 1) * 128,
                                 jg * 512:(jg + 1) * 512], "dD%d" % slot,
                          waits=wl_w)
                v_op(lambda e, so=so: e.tensor_scalar(
                    wdnf[:, so:so + 512], wdnf[:, so:so + 512],
                    scal[:, RSDN:RSDN + 1], 1.49, ALU.mult, ALU.min),
                    waits=[wl, wScal0])
                v_op(lambda e, so=so: e.tensor_scalar(
                    wdnf[:, so:so + 512], wdnf[:, so:so + 512],
                    -1.49, KBIG, ALU.max, ALU.add))
                v_op(lambda e, so=so: e.tensor_scalar(
                    wdnq[:, so:so + 512], wdnf[:, so:so + 512],
                    KBIG, scal[:, SDN:SDN + 1], ALU.subtract, ALU.mult))
                wq = wV()
                for tl in range(4):
                    t_op(lambda e, so=so, tl=tl, hc=hc: e.matmul(
                        pb[2 + tl][:, :],
                        hhf[:, hc * 512 + tl * 128:hc * 512 + (tl + 1) * 128],
                        wdnq[:, so:so + 512],
                        start=(hc == 0), stop=(hc == NH - 1)),
                        waits=[wq, wS()] if (tl == 0 and hc == 0) else
                        ([wq] if tl == 0 else ()),
                        inc="sT" if tl == 3 else None)
                dn_done[idx] = B.cur("sT")
            wDn = wT()
            for tl in range(4):
                s_op(lambda e, tl=tl, jg=jg: e.activation(
                    ost[:, tl * D + jg * 512:tl * D + (jg + 1) * 512],
                    pb[2 + tl][:, :], AF.Copy, scale=1.0),
                    waits=[wDn, wV()], drain=False)
        wEv = wS()
        for tl in range(4):
            tt = half * 4 + tl
            c0 = 8 + tt * 2
            wx = load(xa[:, :], x_cur[tt * 128:(tt + 1) * 128, :], "dX",
                      waits=[wV(), wG()])
            v_op(lambda e, tl=tl: e.tensor_tensor(
                b1[:, :], ost[:, tl * D:(tl + 1) * D], gm_rep[:, :], ALU.mult),
                waits=[wEv, wx])
            ln_apply(b2[:, :], xa[:, :], c0)
            v_op(lambda e: e.tensor_tensor(b1[:, :], b1[:, :], b2[:, :],
                                           ALU.add))
            ln_stats(b1[:, :], 2, EPS_LN)
            ln_apply(b2[:, :], b1[:, :], 2)
            v_op(lambda e: e.tensor_tensor(xa[:, :], xa[:, :], b2[:, :],
                                           ALU.add))
            store(x_cur[tt * 128:(tt + 1) * 128, :], xa[:, :], waits=[wV()])

    if stop_after == 2:
        finish_early()
        return _emit(nc, B, sems)
    for tt in range(NT):
        wx = load(xa[:, :], x_cur[tt * 128:(tt + 1) * 128, :], "dX",
                  waits=[wV(), wG()])
        ln_stats(xa[:, :], 0, EPS_LN, waits=[wx])
        ln_apply(bb[:, :], xa[:, :], 0)
        store(cc_u_in[tt * 128:(tt + 1) * 128, :], bb[:, :], waits=[wV()])
    g_op(lambda e: e.collective_compute(
        "AllGather", ALU.bypass, replica_groups=RGP,
        ins=[cc_u_in.tensor.ap().opt()],
        outs=[cc_u_out.tensor.ap().opt()]),
        waits=[wG()], inc="cc")
    cc_ag = B.cur("cc")
    g_op(lambda e: e.wait_ge(sems["cc"], cc_ag))
    dmy = store(t1px[0:1, 48:56], ar_sum_out[0:1, :])

    A = PHASE0
    wst = salloc("wst", [128, ND * 128], F32, A); A += ND * 128 * 4
    qwb = salloc("qwb", [128, 4 * ND * 128], BF16, A); A += 4 * ND * 128 * 2
    kwb = salloc("kwb", [128, 4 * ND * 128], BF16, A); A += 4 * ND * 128 * 2
    vwb = salloc("vwb", [128, 4 * ND * 128], BF16, A); A += 4 * ND * 128 * 2
    uTg = salloc("uTg", [128, ND * 512], BF16, A); A += ND * 512 * 2
    qfm = salloc("qfm", [128, 4 * SQ], BF16, A); A += 4 * SQ * 2
    kfm = salloc("kfm", [128, 4 * SQ], BF16, A); A += 4 * SQ * 2
    vfm = salloc("vfm", [128, 4 * SQ], BF16, A); A += 4 * SQ * 2
    afm = salloc("afm", [128, 4 * SQ], BF16, A); A += 4 * SQ * 2
    vtm = salloc("vtm", [128, NQ * 128], BF16, A); A += NQ * 128 * 2
    psb = salloc("psb", [128, SQ], BF16, A); A += SQ * 2
    ptf = salloc("ptf", [128, NQ * 128], BF16, A); A += NQ * 128 * 2
    A_OPROJ = A

    p1_stored = 0
    for p in range(2):
        for wi, (wdram, wsb) in enumerate([(qw, qwb), (kw, kwb), (vw, vwb)]):
            for jc in range(4):
                col0 = p * 512 + jc * 128
                src3 = wdram[:, col0:col0 + 128].rearrange(
                    "(dc p) f -> p dc f", p=128)
                dst3 = wst[:, :].rearrange("p (dc f) -> p dc f", dc=ND)
                load(dst3[:, 0:8, :], src3[:, 0:8, :], "dW3", waits=[wV()])
                wl2 = load(dst3[:, 8:16, :], src3[:, 8:16, :], "dW3")
                v_op(lambda e, wsb=wsb, jc=jc: e.tensor_copy(
                    wsb[:, jc * ND * 128:(jc + 1) * ND * 128], wst[:, :]),
                    waits=[wl2])
        wWp = B.cur("sV")
        for tg in range(4):
            for i4 in range(4):
                r0 = tg * 512 + i4 * 128
                wl = load(bb[:, :], cc_u_out[r0:r0 + 128, :], "dB",
                          waits=[("dG", dmy), wS(), wV(), wT()])
                for dc in range(ND):
                    transpose(
                        uTg[:, dc * 512 + i4 * 128:dc * 512 + (i4 + 1) * 128],
                        bb[:, dc * 128:(dc + 1) * 128], eyeb[:, :],
                        pb[6][:, 0:128],
                        waits=[wl] if dc == 0 else [wS()])
            wtr = wS()
            for wi, (wsb, ofm) in enumerate([(qwb, qfm), (kwb, kfm),
                                             (vwb, vfm)]):
                for jc in range(4):
                    for dc in range(ND):
                        t_op(lambda e, wsb=wsb, jc=jc, dc=dc, wi=wi: e.matmul(
                            pb[wi][:, :],
                            wsb[:, jc * ND * 128 + dc * 128:
                                jc * ND * 128 + (dc + 1) * 128],
                            uTg[:, dc * 512:(dc + 1) * 512],
                            start=(dc == 0), stop=(dc == ND - 1)),
                            waits=[wtr, ("sV", wWp), wS()] if dc == 0 else (),
                            inc="sT" if dc == ND - 1 else None)
                    s_op(lambda e, ofm=ofm, jc=jc, tg=tg, wi=wi: e.activation(
                        ofm[:, jc * SQ + tg * 512:jc * SQ + (tg + 1) * 512],
                        pb[wi][:, :], AF.Copy, scale=1.0),
                        waits=[wT()], drain=False)
        wProj = wS()
        for hl in range(4):
            for kt in range(NQ):
                transpose(vtm[:, kt * 128:(kt + 1) * 128],
                          vfm[:, hl * SQ + kt * 128:hl * SQ + (kt + 1) * 128],
                          eyeb[:, :], pb[6][:, 0:128],
                          waits=[wProj, wS()] if kt == 0 else [wS()])
            wvt = wS()
            for qt in range(NQ):
                keys = (qt + 1) * 128
                nkg = (keys + 511) // 512
                for kg in range(nkg):
                    w = min(512, keys - kg * 512)
                    t_op(lambda e, hl=hl, qt=qt, kg=kg, w=w: e.matmul(
                        pb[kg][:, 0:w],
                        qfm[:, hl * SQ + qt * 128:hl * SQ + (qt + 1) * 128],
                        kfm[:, hl * SQ + kg * 512:hl * SQ + kg * 512 + w],
                        start=True, stop=True),
                        waits=[wvt, wV()] if kg == 0 else (),
                        inc="sT" if kg == nkg - 1 else None)
                wsc = wT()
                bd, od = qt // 4, (qt % 4) * 128
                v_op(lambda e, bd=bd, od=od: e.tensor_tensor(
                    pb[bd][:, od:od + 128], pb[bd][:, od:od + 128],
                    caus[:, :], ALU.add), waits=[wsc])
                for kg in range(nkg):
                    w = min(512, keys - kg * 512)
                    v_op(lambda e, kg=kg, w=w: e.reduce_max(
                        tiny[:, kg:kg + 1], pb[kg][:, 0:w], axis=AX),
                        drain=False)
                B.drain("vector")
                for kg in range(1, nkg):
                    v_op(lambda e, kg=kg: e.tensor_tensor(
                        tiny[:, 0:1], tiny[:, 0:1], tiny[:, kg:kg + 1],
                        ALU.max))
                v_op(lambda e: e.tensor_scalar(tiny[:, 4:5], tiny[:, 0:1],
                                               -INV_SQRT_D, None, ALU.mult))
                wmx = wV()
                for kg in range(nkg):
                    w = min(512, keys - kg * 512)
                    s_op(lambda e, kg=kg, w=w: e.activation(
                        psb[:, kg * 512:kg * 512 + w], pb[kg][:, 0:w], AF.Exp,
                        bias=tiny[:, 4:5], scale=INV_SQRT_D),
                        waits=[wmx, wT()] if kg == 0 else (), drain=False)
                B.drain("scalar")
                wexp = wS()
                v_op(lambda e, keys=keys: e.reduce_sum(
                    tiny[:, 5:6], psb[:, 0:keys], axis=AX), waits=[wexp])
                v_op(lambda e: e.reciprocal(tiny[:, 6:7], tiny[:, 5:6]))
                v_op(lambda e, keys=keys: e.tensor_scalar(
                    psb[:, 0:keys], psb[:, 0:keys], tiny[:, 6:7], None,
                    ALU.mult))
                wnorm = wV()
                evs = []
                for kt in range(qt + 1):
                    bk = 4 + (kt % 2)
                    w_ = [wnorm] if kt == 0 else \
                        ([("sS", evs[kt - 2])] if kt >= 2 else [])
                    t_op(lambda e, kt=kt, bk=bk: e.matmul(
                        pb[bk][:, 0:128], psb[:, kt * 128:(kt + 1) * 128],
                        eyeb[:, :], start=True, stop=True), waits=w_, inc="sT")
                    s_op(lambda e, kt=kt, bk=bk: e.activation(
                        ptf[:, kt * 128:(kt + 1) * 128], pb[bk][:, 0:128],
                        AF.Copy, scale=1.0), waits=[wT()], drain=False)
                    evs.append(B.cur("sS"))
                for kt in range(qt + 1):
                    t_op(lambda e, kt=kt, qt=qt: e.matmul(
                        pb[6][:, 0:128], vtm[:, kt * 128:(kt + 1) * 128],
                        ptf[:, kt * 128:(kt + 1) * 128],
                        start=(kt == 0), stop=(kt == qt)),
                        waits=[("sS", evs[-1])] if kt == 0 else (),
                        inc="sT" if kt == qt else None)
                ev_w = [wT()] + ([("dG", p1_stored)] if p == 1 else [])
                s_op(lambda e, hl=hl, qt=qt: e.activation(
                    afm[:, hl * SQ + qt * 128:hl * SQ + (qt + 1) * 128],
                    pb[6][:, 0:128], AF.Copy, scale=1.0),
                    waits=ev_w, drain=False)
        wAtt = wS()
        if p == 0:
            for hc in range(4):
                store(attn1_sp[hc, :, :], afm[:, hc * SQ:(hc + 1) * SQ],
                      waits=[wAtt])
            p1_stored = B.cur("dG")

    A = PHASE0
    owb = salloc("owb", [128, 8 * D], BF16, A); A += 8 * D * 2
    a1b = salloc("a1b", [128, 4 * 256], BF16, A); A += 4 * 256 * 2
    pos = salloc("pos", [128, 2 * D], F32, A); A += 2 * D * 4
    assert A <= PHASE0 + (ND * 128 * 4 + 3 * 4 * ND * 128 * 2 + ND * 512 * 2)
    for hc in range(8):
        wl = load(b1[:, :], ow[hc * 128:(hc + 1) * 128, :], "dB",
                  waits=[wV()])
        v_op(lambda e, hc=hc: e.tensor_copy(owb[:, hc * D:(hc + 1) * D],
                                            b1[:, :]), waits=[wl, wT()])
    wOw = B.cur("sV")
    for r in range(8):
        t0 = r * 256
        for hc in range(4):
            load(a1b[:, hc * 256:(hc + 1) * 256],
                 attn1_sp[hc, :, t0:t0 + 256], "dA1",
                 waits=[("dG", p1_stored), wT()])
        wlr = ("dA1", B.cur("dA1"))
        for tl in range(2):
            for hc in range(8):
                if hc < 4:
                    lhs = a1b[:, hc * 256 + tl * 128:hc * 256 + (tl + 1) * 128]
                else:
                    lhs = afm[:, (hc - 4) * SQ + t0 + tl * 128:
                              (hc - 4) * SQ + t0 + (tl + 1) * 128]
                for jg in range(4):
                    t_op(lambda e, lhs=lhs, jg=jg, hc=hc, tl=tl: e.matmul(
                        pb[tl * 4 + jg][:, :], lhs,
                        owb[:, hc * D + jg * 512:hc * D + (jg + 1) * 512],
                        start=(hc == 0), stop=(hc == 7)),
                        waits=[wlr, ("sV", wOw), wS()]
                        if (hc == 0 and jg == 0) else (),
                        inc="sT" if (hc == 7 and jg == 3) else None)
        wmm = wT()
        for tl in range(2):
            for jg in range(4):
                s_op(lambda e, tl=tl, jg=jg: e.activation(
                    pos[:, tl * D + jg * 512:tl * D + (jg + 1) * 512],
                    pb[tl * 4 + jg][:, :], AF.Copy, scale=1.0),
                    waits=[wmm, wG()], drain=False)
        B.drain("scalar")
        wev = wS()
        for tl in range(2):
            store(cc_po_in[t0 + tl * 128:t0 + (tl + 1) * 128, :],
                  pos[:, tl * D:(tl + 1) * D], waits=[wev])
    g_op(lambda e: e.collective_compute(
        "ReduceScatter", ALU.add, replica_groups=RGP,
        ins=[cc_po_in.tensor.ap().opt()],
        outs=[cc_po_out.tensor.ap().opt()]),
        waits=[wG()], inc="cc")
    cc_rs = B.cur("cc")
    g_op(lambda e: e.wait_ge(sems["cc"], cc_rs))
    dmy2 = store(t1px[0:1, 48:56], ar_sum_out[0:1, :])
    for tt in range(NT):
        wx = load(xa[:, :], x_cur[tt * 128:(tt + 1) * 128, :], "dX",
                  waits=[wV(), wG()])
        wp = load(b1[:, :], cc_po_out[tt * 128:(tt + 1) * 128, :], "dB",
                  waits=[("dG", dmy2), wV()])
        ln_stats(b1[:, :], 0, EPS_LN, waits=[wp])
        ln_apply(b2[:, :], b1[:, :], 0)
        v_op(lambda e: e.tensor_tensor(xa[:, :], xa[:, :], b2[:, :], ALU.add),
             waits=[wx])
        store(x_cur[tt * 128:(tt + 1) * 128, :], xa[:, :], waits=[wV()])

    if stop_after == 3:
        finish_early()
        return _emit(nc, B, sems)
    A = PHASE0
    x8T = salloc("x8T", [128, ND * T], F8, A); A += ND * T
    h8 = salloc("h8", [128, NH * T], F8, A); A += NH * T
    wf4 = salloc("wf4", [128, 2 * ND * 128], F32, A); A += 2 * ND * 128 * 4
    w84 = salloc("w84", [128, 2 * ND * 128], F8, A); A += 2 * ND * 128
    hsp = salloc("hsp", [128, 2 * T], F32, A); A += 2 * T * 4
    wpf = salloc("wpf", [128, 2 * 512], F32, A); A += 2 * 512 * 4
    wp8 = salloc("wp8", [128, 2 * 512], F8, A); A += 2 * 512
    ob4 = salloc("ob4", [128, NT * D], BF16, A); A += NT * D * 2
    x8m = salloc("x8m", [128, D], F8, A); A += D

    v_op(lambda e: e.memset(tiny[:, 3:4], 0.0))
    for tt in range(NT):
        c0 = 32 + tt * 2
        wx = load(xa[:, :], x_cur[tt * 128:(tt + 1) * 128, :], "dX",
                  waits=[wV(), wG()])
        ln_stats(xa[:, :], c0, EPS_LN, waits=[wx])
        ln_apply(b1[:, :], xa[:, :], c0)
        v_op(lambda e: e.reduce_max(tiny[:, 0:1], b1[:, :], axis=AX,
                                    apply_absolute_value=True))
        v_op(lambda e: e.tensor_tensor(tiny[:, 3:4], tiny[:, 3:4],
                                       tiny[:, 0:1], ALU.max))
    xpart(tiny[:, 3:4], 0, "max", waits=[wV()])
    v_op(lambda e: e.memset(t1px[0:1, 56:64], 0.0))
    v_op(lambda e: e.tensor_copy(t1px[0:1, 56:57], t1px[0:1, 0:1]))
    stx = store(ar_xmax_in[0:1, :], t1px[0:1, 56:64], waits=[wV()])
    g_op(lambda e: e.collective_compute(
        "AllReduce", ALU.max, replica_groups=RG8,
        ins=[ar_xmax_in.tensor.ap().opt()],
        outs=[ar_xmax_out.tensor.ap().opt()]),
        waits=[("dG", stx)], inc="cc")
    cc_x = B.cur("cc")
    g_op(lambda e: e.wait_ge(sems["cc"], cc_x))
    lx = store(t1px[0:1, 32:40], ar_xmax_out[0:1, :])
    v_op(lambda e: e.tensor_scalar(t1px[0:1, 8:9], t1px[0:1, 32:33],
                                   1.0 / FP8_MAX, 1e-6, ALU.mult, ALU.max),
         waits=[("dG", lx)])
    v_op(lambda e: e.reciprocal(t1px[0:1, 9:10], t1px[0:1, 8:9]))
    v_op(lambda e: e.tensor_scalar(t1px[0:1, 9:10], t1px[0:1, 9:10], 0.5,
                                   None, ALU.mult))
    v_op(lambda e: e.tensor_tensor(t1px[0:1, 10:11], t1px[0:1, 8:9],
                                   scal[0:1, SFCW:SFCW + 1], ALU.mult))
    v_op(lambda e: e.tensor_scalar(t1px[0:1, 10:11], t1px[0:1, 10:11], 4.0,
                                   None, ALU.mult))
    bcast_row(scal[:, 8:10], t1px[0:1, 9:11], 2, pb[6], waits=[wV()])
    for tt in range(NT):
        c0 = 32 + tt * 2
        wx = load(xa[:, :], x_cur[tt * 128:(tt + 1) * 128, :], "dX",
                  waits=[wV(), wG()])
        ln_apply(b1[:, :], xa[:, :], c0, waits=[wx, wS()])
        s_op(lambda e: e.activation(x8m[:, :], b1[:, :], AF.Copy,
                                    scale=scal[:, RSX:RSX + 1]),
             waits=[wV()])
        wq8 = wS()
        for dc in range(ND):
            transpose(x8T[:, dc * T + tt * 128:dc * T + (tt + 1) * 128],
                      x8m[:, dc * 128:(dc + 1) * 128], eye8[:, :],
                      pb[0][:, 0:128], waits=[wq8] if dc == 0 else [wS()])
    wQ8 = wS()
    v_op(lambda e: e.memset(tiny[:, 3:4], 0.0))
    fc_done = {}
    for hc in range(NH):
        slot = hc % 2
        so = slot * ND * 128
        wl_w = [wQ8] if hc < 2 else [("sT", fc_done[hc - 2])]
        src3 = w_fc_t[:, hc * 128:(hc + 1) * 128].rearrange(
            "(dc p) f -> p dc f", p=128)
        dst3 = wf4[:, so:so + ND * 128].rearrange("p (dc f) -> p dc f", dc=ND)
        usem = "dU%d" % slot
        load(dst3[:, 0:8, :], src3[:, 0:8, :], usem, waits=wl_w)
        wl2 = load(dst3[:, 8:16, :], src3[:, 8:16, :], usem)
        s_op(lambda e, so=so: e.activation(
            w84[:, so:so + ND * 128], wf4[:, so:so + ND * 128], AF.Copy,
            scale=scal[:, RSFCW:RSFCW + 1]), waits=[wl2])
        wq = wS()
        for tg in range(2):
            for dc in range(ND):
                t_op(lambda e, so=so, tg=tg, dc=dc: e.matmul(
                    pb[tg][:, :], w84[:, so + dc * 128:so + (dc + 1) * 128],
                    x8T[:, dc * T + tg * 512:dc * T + (tg + 1) * 512],
                    start=(dc == 0), stop=(dc == ND - 1)),
                    waits=[wq, wV()] if (tg == 0 and dc == 0) else (),
                    inc="sT" if dc == ND - 1 else None)
        fc_done[hc] = B.cur("sT")
        wmm = wT()
        so2 = slot * T
        s_op(lambda e, so2=so2: e.activation(
            hsp[:, so2:so2 + 512], pb[0][:, :], AF.Relu,
            scale=scal[:, SCFC:SCFC + 1]), waits=[wmm, wG()], drain=False)
        s_op(lambda e, so2=so2: e.activation(
            hsp[:, so2 + 512:so2 + 1024], pb[1][:, :], AF.Relu,
            scale=scal[:, SCFC:SCFC + 1]), waits=(), drain=True)
        v_op(lambda e, so2=so2: e.tensor_tensor(
            hsp[:, so2:so2 + T], hsp[:, so2:so2 + T], hsp[:, so2:so2 + T],
            ALU.mult), waits=[wS()])
        v_op(lambda e, so2=so2: e.reduce_max(tiny[:, 0:1],
                                             hsp[:, so2:so2 + T], axis=AX))
        v_op(lambda e: e.tensor_tensor(tiny[:, 3:4], tiny[:, 3:4],
                                       tiny[:, 0:1], ALU.max))
        store(h_spill[hc, :, :], hsp[:, so2:so2 + T], waits=[wV()])
    xpart(tiny[:, 3:4], 0, "max", waits=[wV()])
    v_op(lambda e: e.memset(t1px[0:1, 56:64], 0.0))
    v_op(lambda e: e.tensor_copy(t1px[0:1, 56:57], t1px[0:1, 0:1]))
    sth = store(ar_hmax_in[0:1, :], t1px[0:1, 56:64], waits=[wV()])
    g_op(lambda e: e.collective_compute(
        "AllReduce", ALU.max, replica_groups=RG8,
        ins=[ar_hmax_in.tensor.ap().opt()],
        outs=[ar_hmax_out.tensor.ap().opt()]),
        waits=[("dG", sth)], inc="cc")
    cc_h = B.cur("cc")
    g_op(lambda e: e.wait_ge(sems["cc"], cc_h))
    lh = store(t1px[0:1, 40:48], ar_hmax_out[0:1, :])
    v_op(lambda e: e.tensor_scalar(t1px[0:1, 8:9], t1px[0:1, 40:41],
                                   1.0 / FP8_MAX, 1e-6, ALU.mult, ALU.max),
         waits=[("dG", lh)])
    v_op(lambda e: e.reciprocal(t1px[0:1, 9:10], t1px[0:1, 8:9]))
    v_op(lambda e: e.tensor_scalar(t1px[0:1, 9:10], t1px[0:1, 9:10], 0.5,
                                   None, ALU.mult))
    v_op(lambda e: e.tensor_tensor(t1px[0:1, 10:11], t1px[0:1, 8:9],
                                   scal[0:1, SPJW:SPJW + 1], ALU.mult))
    v_op(lambda e: e.tensor_scalar(t1px[0:1, 10:11], t1px[0:1, 10:11], 4.0,
                                   None, ALU.mult))
    bcast_row(scal[:, 10:12], t1px[0:1, 9:11], 2, pb[6], waits=[wV()])
    wHs = wG()
    for hc in range(NH):
        wl = load(b1[:, 0:T], h_spill[hc, :, :], "dB", waits=[wHs, wS()])
        s_op(lambda e, hc=hc: e.activation(
            h8[:, hc * T:(hc + 1) * T], b1[:, 0:T], AF.Copy,
            scale=scal[:, RSH:RSH + 1]), waits=[wl], drain=False)
    wH8 = wS()
    pj_done = {}
    for jg in range(4):
        for hc in range(NH):
            slot = hc % 2
            so = slot * 512
            idx = jg * NH + hc
            wl_w = [wH8] if idx < 2 else [("sT", pj_done[idx - 2])]
            wl = load(wpf[:, so:so + 512],
                      w_pj_t[hc * 128:(hc + 1) * 128,
                             jg * 512:(jg + 1) * 512], "dD%d" % slot,
                      waits=wl_w)
            s_op(lambda e, so=so: e.activation(
                wp8[:, so:so + 512], wpf[:, so:so + 512], AF.Copy,
                scale=scal[:, RSPJW:RSPJW + 1]), waits=[wl],
                drain=False)
            wq = wS()
            for tt in range(NT):
                t_op(lambda e, so=so, tt=tt, hc=hc: e.matmul(
                    pb[tt][:, :],
                    h8[:, hc * T + tt * 128:hc * T + (tt + 1) * 128],
                    wp8[:, so:so + 512], start=(hc == 0), stop=(hc == NH - 1)),
                    waits=[wq] if tt == 0 else (),
                    inc="sT" if tt == 7 else None)
            pj_done[idx] = B.cur("sT")
        wmm = wT()
        for tt in range(NT):
            s_op(lambda e, tt=tt, jg=jg: e.activation(
                ob4[:, tt * D + jg * 512:tt * D + (jg + 1) * 512],
                pb[tt][:, :], AF.Copy, scale=scal[:, SCPJ:SCPJ + 1]),
                waits=[wmm], drain=False)
    B.drain("scalar")
    wEv4 = wS()
    for tt in range(NT):
        wx = load(xa[:, :], x_cur[tt * 128:(tt + 1) * 128, :], "dX",
                  waits=[wV(), wG()])
        ln_stats(ob4[:, tt * D:(tt + 1) * D], 0, EPS_LN,
                 waits=[wEv4, wx])
        ln_apply(b2[:, :], ob4[:, tt * D:(tt + 1) * D], 0)
        v_op(lambda e: e.tensor_tensor(xa[:, :], xa[:, :], b2[:, :], ALU.add))
        store(y_out[tt * 128:(tt + 1) * 128, :], xa[:, :], waits=[wV()])
    gfin = B.cur("dG")
    g_op(lambda e, g=gfin: e.wait_ge(sems["dG"], g))

    return _emit(nc, B, sems)


def _emit(nc, B, sems):
    with nc.Block() as block:
        def emit_for(name):
            def run(eng):
                hw = {}
                for fn, waits, inc, amt in B.ops[name]:
                    for s, tval in waits:
                        if tval > 0 and hw.get(s, -1) < tval:
                            eng.wait_ge(sems[s], tval)
                            hw[s] = tval
                    instr = fn(eng)
                    if inc is not None:
                        instr.then_inc(sems[inc], amt)
            return run

        block.sync(emit_for("sync"))
        block.tensor(emit_for("tensor"))
        block.vector(emit_for("vector"))
        block.scalar(emit_for("scalar"))
        block.gpsimd(emit_for("gpsimd"))

    return nc


_NC_CACHE = [None]


def kernel(**inputs):
    f32 = np.float32
    x = np.ascontiguousarray(np.asarray(inputs["L0_attn_x"],
                                        dtype=f32).reshape(8192, D))
    Wkv = np.asarray(inputs["L0_attn_Wkv"], dtype=f32)
    bkv = np.asarray(inputs["L0_attn_bkv"], dtype=f32)
    wv_t = np.ascontiguousarray(Wkv[128:256].T)
    bv_rep = np.ascontiguousarray(np.broadcast_to(bkv[128:256][None, :],
                                                  (128, 128)))
    nw = np.ascontiguousarray(np.asarray(inputs["L0_ffn_norm_weight"],
                                         dtype=f32)[None, :])
    gm = np.ascontiguousarray(np.asarray(inputs["L0_ffn_gamma"],
                                         dtype=f32)[None, :])
    w_up_t = np.ascontiguousarray(np.asarray(inputs["L0_ffn_weight_up"],
                                             dtype=f32).T)
    w_dn_t = np.ascontiguousarray(np.asarray(inputs["L0_ffn_weight_down"],
                                             dtype=f32).T)
    qwei = np.asarray(inputs["L1_attn_q_wei"], dtype=f32)
    kwei = np.asarray(inputs["L1_attn_k_wei"], dtype=f32)
    vwei = np.asarray(inputs["L1_attn_v_wei"], dtype=f32)
    owei = np.asarray(inputs["L1_attn_out_wei"], dtype=f32)
    w_fc_t = np.ascontiguousarray(np.asarray(inputs["L1_ffn_w_fc"],
                                             dtype=f32).T)
    w_pj_t = np.ascontiguousarray(np.asarray(inputs["L1_ffn_w_proj"],
                                             dtype=f32).T)

    import ml_dtypes
    eye_bf = np.eye(128, dtype=ml_dtypes.bfloat16)
    eye_f8 = np.eye(128, dtype=ml_dtypes.float8_e4m3fn)
    eye_f32 = np.eye(128, dtype=f32)
    ones_row = np.ones((1, 128), dtype=f32)
    causal = np.ascontiguousarray(
        np.triu(np.full((128, 128), -66504.0, dtype=f32), k=1))

    import os
    sa = int(os.environ.get("KSTOP", "99"))
    if _NC_CACHE[0] is None:
        _NC_CACHE[0] = build_nc(sa)
    nc = _NC_CACHE[0]

    in_maps = []
    for c in range(8):
        hsl = slice((c % 2) * 1024, (c % 2) * 1024 + 1024)
        in_maps.append({
            "x_in": np.ascontiguousarray(x[c * 1024:(c + 1) * 1024]),
            "wv_t": wv_t, "bv_rep": bv_rep, "nw_in": nw, "gm_in": gm,
            "w_up_t": w_up_t, "w_dn_t": w_dn_t,
            "qw": np.ascontiguousarray(qwei[:, hsl]),
            "kw": np.ascontiguousarray(kwei[:, hsl]),
            "vw": np.ascontiguousarray(vwei[:, hsl]),
            "ow": np.ascontiguousarray(owei[hsl, :]),
            "w_fc_t": w_fc_t, "w_pj_t": w_pj_t,
            "st_up": np.ascontiguousarray(
                w_up_t.reshape(-1)[c * 2097152:(c + 1) * 2097152]
                .reshape(256, HID)),
            "st_dn": np.ascontiguousarray(
                w_dn_t.reshape(-1)[c * 2097152:(c + 1) * 2097152]
                .reshape(1024, D)),
            "st_fc": np.ascontiguousarray(
                w_fc_t.reshape(-1)[c * 2097152:(c + 1) * 2097152]
                .reshape(256, HID)),
            "st_pj": np.ascontiguousarray(
                w_pj_t.reshape(-1)[c * 2097152:(c + 1) * 2097152]
                .reshape(1024, D)),
            "eye_bf": eye_bf, "eye_f8": eye_f8, "eye_f32": eye_f32,
            "ones_row": ones_row, "causal": causal,
        })
    res = run_bass_kernel_spmd(nc, in_maps, core_ids=list(range(8)))
    out = np.concatenate([np.asarray(r["y_out"]) for r in res.results], axis=0)
    return out.reshape(4, 2048, 2048)


# version 03 (final)
# speedup vs baseline: 1.1902x; 1.1402x over previous
import numpy as np
import concourse.bass as bass
import concourse.mybir as mybir
from concourse.bass_utils import run_bass_kernel_spmd

F32 = mybir.dt.float32
BF16 = mybir.dt.bfloat16
F8 = mybir.dt.float8e4
AF = mybir.ActivationFunctionType
ALU = mybir.AluOpType
AX = mybir.AxisListType.X

T = 1024
NT = 8
D = 2048
ND = 16
HID = 8192
NH = 64
SQ = 2048
NQ = 16
KBIG = 12582912.0
INV_SQRT_D = 0.08838834764831845
EPS_LN = 1e-5
EPS_RMS = 1e-6
EPS_Q = 1e-8
QB = 127.0
FP8_MAX = 448.0

ENGINES = ("sync", "tensor", "vector", "scalar", "gpsimd")


class Build:
    def __init__(self):
        self.ops = {e: [] for e in ENGINES}
        self.ctr = {}

    def cur(self, s):
        return self.ctr.get(s, 0)

    def op(self, eng, fn, waits=(), inc=None, amt=1):
        self.ops[eng].append((fn, tuple(waits), inc, amt))
        if inc is not None:
            self.ctr[inc] = self.ctr.get(inc, 0) + amt
        return self.cur(inc) if inc else None

    def drain(self, eng):
        self.ops[eng].append(((lambda e: e.drain()), (), None, 0))


def build_nc(stop_after=99):
    nc = bass.Bass("TRN2", target_bir_lowering=False, debug=False)

    def din(name, shape, dt=F32):
        return nc.dram_tensor(name, shape, dt, kind="ExternalInput").ap()

    x_in = din("x_in", [T, D])
    wv_t = din("wv_t", [D, 128])
    bv_rep = din("bv_rep", [128, 128])
    nw_in = din("nw_in", [1, D])
    gm_in = din("gm_in", [1, D])
    w_up_t = din("w_up_t", [D, HID])
    w_dn_t = din("w_dn_t", [HID, D])
    qw = din("qw", [D, T])
    kw = din("kw", [D, T])
    vw = din("vw", [D, T])
    ow = din("ow", [T, D])
    w_fc_t = din("w_fc_t", [D, HID])
    w_pj_t = din("w_pj_t", [HID, D])
    st_up = din("st_up", [256, HID])
    st_dn = din("st_dn", [1024, D])
    st_fc = din("st_fc", [256, HID])
    st_pj = din("st_pj", [1024, D])
    eye_bf = din("eye_bf", [128, 128], BF16)
    eye_f8 = din("eye_f8", [128, 128], F8)
    eye_f32 = din("eye_f32", [128, 128])
    ones_row = din("ones_row", [1, 128])
    causal = din("causal", [128, 128])

    y_out = nc.dram_tensor("y_out", [T, D], F32, kind="ExternalOutput").ap()

    def idram(name, shape, dt=F32):
        return nc.dram_tensor(name, shape, dt).ap()

    x_cur = idram("x_cur", [T, D])
    cc_u_in = idram("cc_u_in", [T, D], BF16)
    cc_u_out = idram("cc_u_out", [SQ, D], BF16)
    cc_po_in = idram("cc_po_in", [SQ, D])
    cc_po_out = idram("cc_po_out", [T, D])
    ar_sum_in = idram("ar_sum_in", [1, 8])
    ar_sum_out = idram("ar_sum_out", [1, 8])
    ar_wmax_in = idram("ar_wmax_in", [1, 8])
    ar_wmax_out = idram("ar_wmax_out", [1, 8])
    ar_xmax_in = idram("ar_xmax_in", [1, 8])
    ar_xmax_out = idram("ar_xmax_out", [1, 8])
    ar_hmax_in = idram("ar_hmax_in", [1, 8])
    ar_hmax_out = idram("ar_hmax_out", [1, 8])
    h_spill = idram("h_spill", [NH, 128, T])
    attn1_sp = idram("attn1_sp", [4, 128, SQ], BF16)

    RG8 = [list(range(8))]
    RGP = [[0, 1], [2, 3], [4, 5], [6, 7]]

    B = Build()
    sems = {}
    for s in ("dC", "dSt", "dWv", "dX", "dB", "dU0", "dU1", "dD0", "dD1",
              "dD2", "dD3", "dW3", "dA1", "dG", "cc", "sT", "sV", "sS"):
        cm = nc.semaphore(s)
        sems[s] = cm.__enter__()

    OFS = [16512]

    def salloc(name, shape, dt, ofs=None):
        elt = {F32: 4, BF16: 2, F8: 1}[dt]
        nbytes = int(np.prod(shape[1:])) * elt
        if ofs is None:
            ofs = OFS[0]
            OFS[0] += (nbytes + 31) & ~31
        assert ofs + nbytes <= 229344, (name, ofs, nbytes)
        return nc.alloc_sbuf_tensor_at(name, shape, dt, offset=ofs)

    eyeb = salloc("eyeb", [128, 128], BF16)
    eye8 = salloc("eye8", [128, 128], F8)
    eyef = salloc("eyef", [128, 128], F32)
    onesr = salloc("onesr", [1, 128], F32)
    caus = salloc("caus", [128, 128], F32)
    bvr = salloc("bvr", [128, 128], F32)
    scal = salloc("scal", [128, 16], F32)
    stat = salloc("stat", [128, 64], F32)
    tiny = salloc("tiny", [128, 32], F32)
    t1px = salloc("t1px", [1, 64], F32)
    nw_rep = salloc("nw_rep", [128, D], F32)
    gm_rep = salloc("gm_rep", [128, D], F32)
    xa = salloc("xa", [128, D], F32)
    b1 = salloc("b1", [128, D], F32)
    b2 = salloc("b2", [128, D], F32)
    bb = salloc("bb", [128, D], BF16)
    csts = salloc("csts", [128, 2], F32)
    PHASE0 = OFS[0]

    pbS = nc.alloc_psum_tensor("pbS", [128, 2048], F32)
    pb = [pbS[:, i * 512:(i + 1) * 512] for i in range(4)] + \
        [nc.alloc_psum_tensor("pb%d" % i, [128, 512], F32) for i in range(4, 8)]

    SUP, SDN, RSUP, RSDN, SFCW, SPJW, RSFCW, RSPJW = 0, 1, 2, 3, 4, 5, 6, 7
    RSX, SCFC, RSH, SCPJ = 8, 9, 10, 11

    def load(dst, src, sem, waits=()):
        c = B.op("sync", lambda e, d=dst, s=src: e.dma_start(out=d, in_=s),
                 waits=waits, inc=sem, amt=16)
        return (sem, c)

    def store(dst, src, waits=()):
        return B.op("gpsimd", lambda e, d=dst, s=src: e.dma_start(out=d, in_=s),
                    waits=waits, inc="dG", amt=16)

    def v_op(fn, waits=(), drain=True):
        r = B.op("vector", fn, waits=waits, inc="sV")
        if drain:
            B.drain("vector")
        return r

    def s_op(fn, waits=(), drain=True):
        r = B.op("scalar", fn, waits=waits, inc="sS")
        if drain:
            B.drain("scalar")
        return r

    def t_op(fn, waits=(), inc=None):
        return B.op("tensor", fn, waits=waits, inc=inc)

    def g_op(fn, waits=(), inc=None, amt=1):
        return B.op("gpsimd", fn, waits=waits, inc=inc, amt=amt)

    def wV():
        return ("sV", B.cur("sV"))

    def wS():
        return ("sS", B.cur("sS"))

    def wT():
        return ("sT", B.cur("sT"))

    def wL():
        return ("dL", B.cur("dL"))

    def wG():
        return ("dG", B.cur("dG"))


    def finish_early():
        for tt in range(NT):
            wx = load(xa[:, :], x_cur[tt * 128:(tt + 1) * 128, :], "dX",
                      waits=[wV(), wG()])
            v_op(lambda e: e.tensor_copy(b2[:, :], xa[:, :]), waits=[wx])
            store(y_out[tt * 128:(tt + 1) * 128, :], b2[:, :], waits=[wV()])
        gfin = B.cur("dG")
        g_op(lambda e, g=gfin: e.wait_ge(sems["dG"], g))

    def bn_mv(src, waits=(), width=D):
        ng = max(1, width // 512)
        cw = width // ng
        for g in range(ng):
            v_op(lambda e, g=g, cw=cw: e.bn_stats(
                tiny[:, g * 6:(g + 1) * 6], src[:, g * cw:(g + 1) * cw]),
                waits=waits if g == 0 else (), drain=(g == ng - 1))
        v_op(lambda e, ng=ng: e.bn_aggr(tiny[:, 24:26], tiny[:, 0:6 * ng]))

    def ln_stats(src, c0, eps, waits=(), width=D):
        bn_mv(src, waits=waits, width=width)
        v_op(lambda e, c=c0: e.tensor_copy(stat[:, c:c + 1], tiny[:, 24:25]))
        s_op(lambda e: e.activation(tiny[:, 26:27], tiny[:, 25:26], AF.Sqrt,
                                    bias=float(eps), scale=1.0), waits=[wV()])
        v_op(lambda e, c=c0: e.reciprocal(stat[:, c + 1:c + 2], tiny[:, 26:27]),
             waits=[wS()])

    def ln_apply(dst, src, c0, waits=()):
        v_op(lambda e, c=c0: e.tensor_scalar(dst, src, stat[:, c:c + 1],
                                             stat[:, c + 1:c + 2],
                                             ALU.subtract, ALU.mult), waits=waits)

    def transpose(dst, src, eye, ps, waits=()):
        t_op(lambda e: e.matmul(ps, src, eye, start=True, stop=True),
             waits=waits, inc="sT")
        s_op(lambda e: e.activation(dst, ps, AF.Copy, scale=1.0),
             waits=[wT()], drain=False)

    def bcast_row(dst, src_row, n, ps, waits=()):
        t_op(lambda e: e.matmul(ps[:, 0:n], onesr[0:1, 0:128], src_row,
                                start=True, stop=True),
             waits=list(waits) + [wS()], inc="sT")
        s_op(lambda e: e.activation(dst, ps[:, 0:n], AF.Copy, scale=1.0),
             waits=[wT()])

    def xpart(col, dcol, op, waits=()):
        t_op(lambda e: e.matmul(pb[7][0:1, 0:128], col, eyef[:, :],
                                start=True, stop=True), waits=waits, inc="sT")
        if op == "max":
            v_op(lambda e, d=dcol: e.reduce_max(t1px[0:1, d:d + 1],
                                                pb[7][0:1, 0:128], axis=AX),
                 waits=[wT()])
        else:
            v_op(lambda e, d=dcol: e.reduce_sum(t1px[0:1, d:d + 1],
                                                pb[7][0:1, 0:128], axis=AX),
                 waits=[wT()])

    v_op(lambda e: e.memset(csts[:, 0:1], EPS_LN), drain=False)
    v_op(lambda e: e.memset(csts[:, 1:2], EPS_RMS))
    nc.const_aps.register(F32, float(EPS_LN), csts[:, 0:1])
    nc.const_aps.register(F32, float(EPS_RMS), csts[:, 1:2])
    load(eyeb[:, :], eye_bf[:, :], "dC")
    load(eye8[:, :], eye_f8[:, :], "dC")
    load(eyef[:, :], eye_f32[:, :], "dC")
    load(onesr[:, :], ones_row[:, :], "dC")
    load(caus[:, :], causal[:, :], "dC")
    load(bvr[:, :], bv_rep[:, :], "dC")
    load(b1[0:1, 0:D], nw_in[:, :], "dC")
    load(b2[0:1, 0:D], gm_in[:, :], "dC")
    wconst = B.cur("dC")
    for g in range(4):
        bcast_row(nw_rep[:, g * 512:(g + 1) * 512],
                  b1[0:1, g * 512:(g + 1) * 512], 512, pb[g],
                  waits=[("dC", wconst)])
        bcast_row(gm_rep[:, g * 512:(g + 1) * 512],
                  b2[0:1, g * 512:(g + 1) * 512], 512, pb[g],
                  waits=[("dC", wconst)])

    stt = salloc("stt", [128, 2048], F32, PHASE0)
    v_op(lambda e: e.memset(tiny[:, 0:4], 0.0))
    for src, rows, cols, col, op in [
            (st_up, 256, HID, 0, "sum"), (st_dn, 1024, D, 1, "sum"),
            (st_fc, 256, HID, 2, "max"), (st_pj, 1024, D, 3, "max")]:
        for rr in range(rows // 128):
            for ccc in range(cols // 2048):
                wl = load(stt[:, :], src[rr * 128:(rr + 1) * 128,
                                         ccc * 2048:(ccc + 1) * 2048],
                          "dSt", waits=[wV()])
                if op == "sum":
                    v_op(lambda e: e.reduce_sum(tiny[:, 4:5], stt[:, :],
                                                axis=AX,
                                                apply_absolute_value=True),
                         waits=[wl])
                else:
                    v_op(lambda e: e.reduce_max(tiny[:, 4:5], stt[:, :],
                                                axis=AX,
                                                apply_absolute_value=True),
                         waits=[wl])
                aop = ALU.add if op == "sum" else ALU.max
                v_op(lambda e, c=col, o=aop: e.tensor_tensor(
                    tiny[:, c:c + 1], tiny[:, c:c + 1], tiny[:, 4:5], o))
    for col, op in [(0, "sum"), (1, "sum"), (2, "max"), (3, "max")]:
        xpart(tiny[:, col:col + 1], col, op, waits=[wV()])
        B.drain("vector")
    v_op(lambda e: e.memset(t1px[0:1, 16:32], 0.0))
    v_op(lambda e: e.tensor_copy(t1px[0:1, 16:18], t1px[0:1, 0:2]))
    v_op(lambda e: e.tensor_copy(t1px[0:1, 24:26], t1px[0:1, 2:4]))
    store(ar_sum_in[0:1, :], t1px[0:1, 16:24], waits=[wV()])
    st2 = store(ar_wmax_in[0:1, :], t1px[0:1, 24:32])
    g_op(lambda e: e.collective_compute(
        "AllReduce", ALU.add, replica_groups=RG8,
        ins=[ar_sum_in.tensor.ap().opt()],
        outs=[ar_sum_out.tensor.ap().opt()]),
        waits=[("dG", st2)], inc="cc")
    g_op(lambda e: e.collective_compute(
        "AllReduce", ALU.max, replica_groups=RG8,
        ins=[ar_wmax_in.tensor.ap().opt()],
        outs=[ar_wmax_out.tensor.ap().opt()]),
        inc="cc")
    cc_w = B.cur("cc")

    A = PHASE0
    wvf = salloc("wvf", [128, ND * 128], F32, A); A += ND * 128 * 4
    wvb = salloc("wvb", [128, ND * 128], BF16, A); A += ND * 128 * 2
    uT1 = salloc("uT1", [128, ND * 128], BF16, A); A += ND * 128 * 2
    vsb = salloc("vsb", [128, 128], F32, A); A += 512
    lnv = salloc("lnv", [128, 128], F32, A); A += 512
    for dc in range(ND):
        load(wvf[:, dc * 128:(dc + 1) * 128], wv_t[dc * 128:(dc + 1) * 128, :],
             "dWv", waits=[wV()])
    v_op(lambda e: e.tensor_copy(wvb[:, :], wvf[:, :]),
         waits=[("dWv", B.cur("dWv"))])
    wWv = B.cur("sV")
    for tt in range(NT):
        wx = load(xa[:, :], x_in[tt * 128:(tt + 1) * 128, :], "dX",
                  waits=[wG(), wV()])
        ln_stats(xa[:, :], 0, EPS_LN, waits=[wx])
        ln_apply(bb[:, :], xa[:, :], 0, waits=[wT()])
        wU = wV()
        for dc in range(ND):
            transpose(uT1[:, dc * 128:(dc + 1) * 128],
                      bb[:, dc * 128:(dc + 1) * 128], eyeb[:, :],
                      pb[0][:, 0:128], waits=[wU] if dc == 0 else [wS()])
        wtr = wS()
        for dc in range(ND):
            t_op(lambda e, dc=dc: e.matmul(
                pb[1][:, 0:128], uT1[:, dc * 128:(dc + 1) * 128],
                wvb[:, dc * 128:(dc + 1) * 128],
                start=(dc == 0), stop=(dc == ND - 1)),
                waits=[wtr, ("sV", wWv)] if dc == 0 else (),
                inc="sT" if dc == ND - 1 else None)
        v_op(lambda e: e.tensor_tensor(vsb[:, :], pb[1][:, 0:128], bvr[:, :],
                                       ALU.add), waits=[wT()])
        ln_stats(vsb[:, :], 2, EPS_LN, width=128)
        ln_apply(lnv[:, :], vsb[:, :], 2)
        for dc in range(ND):
            v_op(lambda e, dc=dc: e.tensor_tensor(
                xa[:, dc * 128:(dc + 1) * 128], xa[:, dc * 128:(dc + 1) * 128],
                lnv[:, :], ALU.add), drain=False)
        B.drain("vector")
        store(x_cur[tt * 128:(tt + 1) * 128, :], xa[:, :], waits=[wV()])

    if stop_after <= 1:
        finish_early()
    if stop_after <= 1:
        pass
    g_op(lambda e: e.wait_ge(sems["cc"], cc_w))
    store(t1px[0:1, 32:40], ar_sum_out[0:1, :])
    l2 = store(t1px[0:1, 40:48], ar_wmax_out[0:1, :])
    NUP = float(HID * D)
    v_op(lambda e: e.tensor_scalar(t1px[0:1, 0:2], t1px[0:1, 32:34],
                                   1.0 / NUP, EPS_Q, ALU.mult, ALU.max),
         waits=[("dG", l2)])
    v_op(lambda e: e.reciprocal(t1px[0:1, 2:4], t1px[0:1, 0:2]))
    v_op(lambda e: e.tensor_scalar(t1px[0:1, 4:6], t1px[0:1, 40:42],
                                   1.0 / FP8_MAX, 1e-6, ALU.mult, ALU.max))
    v_op(lambda e: e.reciprocal(t1px[0:1, 6:8], t1px[0:1, 4:6]))
    v_op(lambda e: e.tensor_scalar(t1px[0:1, 6:8], t1px[0:1, 6:8], 0.5, None,
                                   ALU.mult))
    bcast_row(scal[:, 0:8], t1px[0:1, 0:8], 8, pb[6], waits=[wV()])
    wScal0 = wS()

    if stop_after <= 1:
        return _emit(nc, B, sems)
    A = PHASE0
    xqT = salloc("xqT", [128, ND * T], BF16, A); A += ND * T * 2
    hhf = salloc("hhf", [128, NH * 512], BF16, A); A += NH * 512 * 2
    wupf = salloc("wupf", [128, 2 * ND * 128], F32, A); A += 2 * ND * 128 * 4
    wupq = salloc("wupq", [128, 2 * ND * 128], BF16, A); A += 2 * ND * 128 * 2
    wdnf = salloc("wdnf", [128, 4 * 512], F32, A); A += 4 * 512 * 4
    wdnq = salloc("wdnq", [128, 4 * 512], BF16, A); A += 4 * 512 * 2
    ost = salloc("ost", [128, 4 * D], BF16, A); A += 4 * D * 2

    for tt in range(NT):
        c0 = 8 + tt * 2
        wx = load(xa[:, :], x_cur[tt * 128:(tt + 1) * 128, :], "dX",
                  waits=[wV(), wG()])
        ln_stats(xa[:, :], c0, EPS_LN, waits=[wx])
        ln_apply(b1[:, :], xa[:, :], c0)
        bn_mv(b1[:, :])
        v_op(lambda e: e.tensor_tensor(tiny[:, 27:28], tiny[:, 24:25],
                                       tiny[:, 24:25], ALU.mult))
        v_op(lambda e: e.tensor_tensor(tiny[:, 28:29], tiny[:, 25:26],
                                       tiny[:, 27:28], ALU.add))
        s_op(lambda e: e.activation(tiny[:, 29:30], tiny[:, 28:29], AF.Sqrt,
                                    bias=EPS_RMS, scale=1.0), waits=[wV()])
        v_op(lambda e: e.reciprocal(tiny[:, 30:31], tiny[:, 29:30]),
             waits=[wS()])
        v_op(lambda e: e.tensor_scalar(b2[:, :], b1[:, :], tiny[:, 30:31],
                                       None, ALU.mult))
        v_op(lambda e: e.tensor_tensor(b2[:, :], b2[:, :], nw_rep[:, :],
                                       ALU.mult))
        v_op(lambda e: e.reduce_max(tiny[:, 0:1], b2[:, :], axis=AX,
                                    apply_absolute_value=True))
        v_op(lambda e: e.tensor_scalar(tiny[:, 1:2], tiny[:, 0:1], EPS_Q,
                                       1.0 / QB, ALU.max, ALU.mult))
        v_op(lambda e: e.reciprocal(tiny[:, 2:3], tiny[:, 1:2]))
        v_op(lambda e: e.tensor_scalar(b2[:, :], b2[:, :], tiny[:, 2:3], KBIG,
                                       ALU.mult, ALU.add))
        v_op(lambda e: e.tensor_scalar(bb[:, :], b2[:, :], KBIG, tiny[:, 1:2],
                                       ALU.subtract, ALU.mult), waits=[wT()])
        wQ = wV()
        for dc in range(ND):
            transpose(xqT[:, dc * T + tt * 128:dc * T + (tt + 1) * 128],
                      bb[:, dc * 128:(dc + 1) * 128], eyeb[:, :],
                      pb[0][:, 0:128], waits=[wQ] if dc == 0 else [wS()])
    wPrep = wS()

    up_done = {}
    dn_done = {}
    for half in range(2):
        tok0 = half * 512
        for hc in range(NH):
            slot = hc % 2
            so = slot * ND * 128
            idx = half * NH + hc
            wl_w = [wPrep] if idx < 2 else [("sT", up_done[idx - 2])]
            src3 = w_up_t[:, hc * 128:(hc + 1) * 128].rearrange(
                "(dc p) f -> p dc f", p=128)
            dst3 = wupf[:, so:so + ND * 128].rearrange(
                "p (dc f) -> p dc f", dc=ND)
            usem = "dU%d" % slot
            load(dst3[:, 0:8, :], src3[:, 0:8, :], usem, waits=wl_w)
            wl2 = load(dst3[:, 8:16, :], src3[:, 8:16, :], usem)
            v_op(lambda e, so=so: e.tensor_scalar(
                wupf[:, so:so + ND * 128], wupf[:, so:so + ND * 128],
                scal[:, RSUP:RSUP + 1], 1.49, ALU.mult, ALU.min),
                waits=[wl2, wScal0])
            v_op(lambda e, so=so: e.tensor_scalar(
                wupf[:, so:so + ND * 128], wupf[:, so:so + ND * 128],
                -1.49, KBIG, ALU.max, ALU.add))
            v_op(lambda e, so=so: e.tensor_scalar(
                wupq[:, so:so + ND * 128], wupf[:, so:so + ND * 128],
                KBIG, scal[:, SUP:SUP + 1], ALU.subtract, ALU.mult))
            wq = wV()
            for dc in range(ND):
                t_op(lambda e, so=so, dc=dc, s=slot: e.matmul(
                    pb[s][:, :], wupq[:, so + dc * 128:so + (dc + 1) * 128],
                    xqT[:, dc * T + tok0:dc * T + tok0 + 512],
                    start=(dc == 0), stop=(dc == ND - 1)),
                    waits=[wq, wS()] if dc == 0 else (),
                    inc="sT" if dc == ND - 1 else None)
            up_done[idx] = B.cur("sT")
            s_op(lambda e, hc=hc, s=slot: e.activation(
                hhf[:, hc * 512:(hc + 1) * 512], pb[s][:, :], AF.Silu,
                scale=1.0), waits=[wT()], drain=False)
        wUp = wS()
        for jg in range(4):
            for hc in range(NH):
                slot = hc % 4
                so = slot * 512
                idx = (half * 4 + jg) * NH + hc
                wl_w = [wUp] if (jg == 0 and hc < 4) else \
                    [("sT", dn_done[idx - 4])]
                wl = load(wdnf[:, so:so + 512],
                          w_dn_t[hc * 128:(hc + 1) * 128,
                                 jg * 512:(jg + 1) * 512], "dD%d" % slot,
                          waits=wl_w)
                v_op(lambda e, so=so: e.tensor_scalar(
                    wdnf[:, so:so + 512], wdnf[:, so:so + 512],
                    scal[:, RSDN:RSDN + 1], 1.49, ALU.mult, ALU.min),
                    waits=[wl, wScal0])
                v_op(lambda e, so=so: e.tensor_scalar(
                    wdnf[:, so:so + 512], wdnf[:, so:so + 512],
                    -1.49, KBIG, ALU.max, ALU.add))
                v_op(lambda e, so=so: e.tensor_scalar(
                    wdnq[:, so:so + 512], wdnf[:, so:so + 512],
                    KBIG, scal[:, SDN:SDN + 1], ALU.subtract, ALU.mult))
                wq = wV()
                for tl in range(4):
                    t_op(lambda e, so=so, tl=tl, hc=hc: e.matmul(
                        pb[2 + tl][:, :],
                        hhf[:, hc * 512 + tl * 128:hc * 512 + (tl + 1) * 128],
                        wdnq[:, so:so + 512],
                        start=(hc == 0), stop=(hc == NH - 1)),
                        waits=[wq, wS()] if (tl == 0 and hc == 0) else
                        ([wq] if tl == 0 else ()),
                        inc="sT" if tl == 3 else None)
                dn_done[idx] = B.cur("sT")
            wDn = wT()
            for tl in range(4):
                s_op(lambda e, tl=tl, jg=jg: e.activation(
                    ost[:, tl * D + jg * 512:tl * D + (jg + 1) * 512],
                    pb[2 + tl][:, :], AF.Copy, scale=1.0),
                    waits=[wDn, wV()], drain=False)
        wEv = wS()
        for tl in range(4):
            tt = half * 4 + tl
            c0 = 8 + tt * 2
            wx = load(xa[:, :], x_cur[tt * 128:(tt + 1) * 128, :], "dX",
                      waits=[wV(), wG()])
            v_op(lambda e, tl=tl: e.tensor_tensor(
                b1[:, :], ost[:, tl * D:(tl + 1) * D], gm_rep[:, :], ALU.mult),
                waits=[wEv, wx])
            ln_apply(b2[:, :], xa[:, :], c0)
            v_op(lambda e: e.tensor_tensor(b1[:, :], b1[:, :], b2[:, :],
                                           ALU.add))
            ln_stats(b1[:, :], 2, EPS_LN)
            ln_apply(b2[:, :], b1[:, :], 2)
            v_op(lambda e: e.tensor_tensor(xa[:, :], xa[:, :], b2[:, :],
                                           ALU.add))
            store(x_cur[tt * 128:(tt + 1) * 128, :], xa[:, :], waits=[wV()])

    if stop_after == 2:
        finish_early()
        return _emit(nc, B, sems)
    for tt in range(NT):
        wx = load(xa[:, :], x_cur[tt * 128:(tt + 1) * 128, :], "dX",
                  waits=[wV(), wG()])
        ln_stats(xa[:, :], 0, EPS_LN, waits=[wx])
        ln_apply(bb[:, :], xa[:, :], 0)
        store(cc_u_in[tt * 128:(tt + 1) * 128, :], bb[:, :], waits=[wV()])
    g_op(lambda e: e.collective_compute(
        "AllGather", ALU.bypass, replica_groups=RGP,
        ins=[cc_u_in.tensor.ap().opt()],
        outs=[cc_u_out.tensor.ap().opt()]),
        waits=[wG()], inc="cc")
    cc_ag = B.cur("cc")
    g_op(lambda e: e.wait_ge(sems["cc"], cc_ag))
    dmy = store(t1px[0:1, 48:56], ar_sum_out[0:1, :])

    A = PHASE0
    wst = salloc("wst", [128, ND * 128], F32, A); A += ND * 128 * 4
    qwb = salloc("qwb", [128, 4 * ND * 128], BF16, A); A += 4 * ND * 128 * 2
    kwb = salloc("kwb", [128, 4 * ND * 128], BF16, A); A += 4 * ND * 128 * 2
    vwb = salloc("vwb", [128, 4 * ND * 128], BF16, A); A += 4 * ND * 128 * 2
    uTg = salloc("uTg", [128, ND * 512], BF16, A); A += ND * 512 * 2
    qfm = salloc("qfm", [128, 4 * SQ], BF16, A); A += 4 * SQ * 2
    kfm = salloc("kfm", [128, 4 * SQ], BF16, A); A += 4 * SQ * 2
    vfm = salloc("vfm", [128, 4 * SQ], BF16, A); A += 4 * SQ * 2
    afm = salloc("afm", [128, 4 * SQ], BF16, A); A += 4 * SQ * 2
    vtm = salloc("vtm", [128, NQ * 128], BF16, A); A += NQ * 128 * 2
    psb = salloc("psb", [128, SQ], BF16, A); A += SQ * 2
    ptf = salloc("ptf", [128, NQ * 128], BF16, A); A += NQ * 128 * 2
    A_OPROJ = A

    p1_stored = 0
    for p in range(2):
        for wi, (wdram, wsb) in enumerate([(qw, qwb), (kw, kwb), (vw, vwb)]):
            for jc in range(4):
                col0 = p * 512 + jc * 128
                src3 = wdram[:, col0:col0 + 128].rearrange(
                    "(dc p) f -> p dc f", p=128)
                dst3 = wst[:, :].rearrange("p (dc f) -> p dc f", dc=ND)
                load(dst3[:, 0:8, :], src3[:, 0:8, :], "dW3", waits=[wV()])
                wl2 = load(dst3[:, 8:16, :], src3[:, 8:16, :], "dW3")
                v_op(lambda e, wsb=wsb, jc=jc: e.tensor_copy(
                    wsb[:, jc * ND * 128:(jc + 1) * ND * 128], wst[:, :]),
                    waits=[wl2])
        wWp = B.cur("sV")
        for tg in range(4):
            for i4 in range(4):
                r0 = tg * 512 + i4 * 128
                wl = load(bb[:, :], cc_u_out[r0:r0 + 128, :], "dB",
                          waits=[("dG", dmy), wS(), wV(), wT()])
                for dc in range(ND):
                    transpose(
                        uTg[:, dc * 512 + i4 * 128:dc * 512 + (i4 + 1) * 128],
                        bb[:, dc * 128:(dc + 1) * 128], eyeb[:, :],
                        pb[6][:, 0:128],
                        waits=[wl] if dc == 0 else [wS()])
            wtr = wS()
            for wi, (wsb, ofm) in enumerate([(qwb, qfm), (kwb, kfm),
                                             (vwb, vfm)]):
                for jc in range(4):
                    for dc in range(ND):
                        t_op(lambda e, wsb=wsb, jc=jc, dc=dc, wi=wi: e.matmul(
                            pb[wi][:, :],
                            wsb[:, jc * ND * 128 + dc * 128:
                                jc * ND * 128 + (dc + 1) * 128],
                            uTg[:, dc * 512:(dc + 1) * 512],
                            start=(dc == 0), stop=(dc == ND - 1)),
                            waits=[wtr, ("sV", wWp), wS()] if dc == 0 else (),
                            inc="sT" if dc == ND - 1 else None)
                    s_op(lambda e, ofm=ofm, jc=jc, tg=tg, wi=wi: e.activation(
                        ofm[:, jc * SQ + tg * 512:jc * SQ + (tg + 1) * 512],
                        pb[wi][:, :], AF.Copy, scale=1.0),
                        waits=[wT()], drain=False)
        wProj = wS()
        for hl in range(4):
            for kt in range(NQ):
                transpose(vtm[:, kt * 128:(kt + 1) * 128],
                          vfm[:, hl * SQ + kt * 128:hl * SQ + (kt + 1) * 128],
                          eyeb[:, :], pb[6][:, 0:128],
                          waits=[wProj, wS()] if kt == 0 else [wS()])
            wvt = wS()
            for qt in range(NQ):
                keys = (qt + 1) * 128
                nkg = (keys + 511) // 512
                for kg in range(nkg):
                    w = min(512, keys - kg * 512)
                    t_op(lambda e, hl=hl, qt=qt, kg=kg, w=w: e.matmul(
                        pb[kg][:, 0:w],
                        qfm[:, hl * SQ + qt * 128:hl * SQ + (qt + 1) * 128],
                        kfm[:, hl * SQ + kg * 512:hl * SQ + kg * 512 + w],
                        start=True, stop=True),
                        waits=[wvt, wV()] if kg == 0 else (),
                        inc="sT" if kg == nkg - 1 else None)
                wsc = wT()
                qoff = qt * 128
                v_op(lambda e, qoff=qoff: e.tensor_tensor(
                    pbS[:, qoff:qoff + 128], pbS[:, qoff:qoff + 128],
                    caus[:, :], ALU.add), waits=[wsc])
                v_op(lambda e, keys=keys: e.reduce_max(
                    tiny[:, 0:1], pbS[:, 0:keys], axis=AX, negate=True))
                v_op(lambda e: e.tensor_scalar(tiny[:, 4:5], tiny[:, 0:1],
                                               INV_SQRT_D, None, ALU.mult))
                wmx = wV()
                s_op(lambda e, keys=keys: e.activation(
                    psb[:, 0:keys], pbS[:, 0:keys], AF.Exp,
                    bias=tiny[:, 4:5], scale=INV_SQRT_D),
                    waits=[wmx, wT()], drain=False)
                B.drain("scalar")
                wexp = wS()
                v_op(lambda e, keys=keys: e.reduce_sum(
                    tiny[:, 5:6], psb[:, 0:keys], axis=AX), waits=[wexp])
                v_op(lambda e: e.reciprocal(tiny[:, 6:7], tiny[:, 5:6]))
                v_op(lambda e, keys=keys: e.tensor_scalar(
                    psb[:, 0:keys], psb[:, 0:keys], tiny[:, 6:7], None,
                    ALU.mult))
                wnorm = wV()
                evs = []
                for kt in range(qt + 1):
                    bk = 4 + (kt % 2)
                    w_ = [wnorm] if kt == 0 else \
                        ([("sS", evs[kt - 2])] if kt >= 2 else [])
                    t_op(lambda e, kt=kt, bk=bk: e.matmul(
                        pb[bk][:, 0:128], psb[:, kt * 128:(kt + 1) * 128],
                        eyeb[:, :], start=True, stop=True), waits=w_, inc="sT")
                    s_op(lambda e, kt=kt, bk=bk: e.activation(
                        ptf[:, kt * 128:(kt + 1) * 128], pb[bk][:, 0:128],
                        AF.Copy, scale=1.0), waits=[wT()], drain=False)
                    evs.append(B.cur("sS"))
                for kt in range(qt + 1):
                    t_op(lambda e, kt=kt, qt=qt: e.matmul(
                        pb[6][:, 0:128], vtm[:, kt * 128:(kt + 1) * 128],
                        ptf[:, kt * 128:(kt + 1) * 128],
                        start=(kt == 0), stop=(kt == qt)),
                        waits=[("sS", evs[-1])] if kt == 0 else (),
                        inc="sT" if kt == qt else None)
                ev_w = [wT()] + ([("dG", p1_stored)] if p == 1 else [])
                s_op(lambda e, hl=hl, qt=qt: e.activation(
                    afm[:, hl * SQ + qt * 128:hl * SQ + (qt + 1) * 128],
                    pb[6][:, 0:128], AF.Copy, scale=1.0),
                    waits=ev_w, drain=False)
        wAtt = wS()
        if p == 0:
            for hc in range(4):
                store(attn1_sp[hc, :, :], afm[:, hc * SQ:(hc + 1) * SQ],
                      waits=[wAtt])
            p1_stored = B.cur("dG")

    A = PHASE0
    owb = salloc("owb", [128, 8 * D], BF16, A); A += 8 * D * 2
    a1b = salloc("a1b", [128, 4 * 256], BF16, A); A += 4 * 256 * 2
    pos = salloc("pos", [128, 2 * D], F32, A); A += 2 * D * 4
    assert A <= PHASE0 + (ND * 128 * 4 + 3 * 4 * ND * 128 * 2 + ND * 512 * 2)
    for hc in range(8):
        wl = load(b1[:, :], ow[hc * 128:(hc + 1) * 128, :], "dB",
                  waits=[wV()])
        v_op(lambda e, hc=hc: e.tensor_copy(owb[:, hc * D:(hc + 1) * D],
                                            b1[:, :]), waits=[wl, wT()])
    wOw = B.cur("sV")
    for r in range(8):
        t0 = r * 256
        for hc in range(4):
            load(a1b[:, hc * 256:(hc + 1) * 256],
                 attn1_sp[hc, :, t0:t0 + 256], "dA1",
                 waits=[("dG", p1_stored), wT()])
        wlr = ("dA1", B.cur("dA1"))
        for tl in range(2):
            for hc in range(8):
                if hc < 4:
                    lhs = a1b[:, hc * 256 + tl * 128:hc * 256 + (tl + 1) * 128]
                else:
                    lhs = afm[:, (hc - 4) * SQ + t0 + tl * 128:
                              (hc - 4) * SQ + t0 + (tl + 1) * 128]
                for jg in range(4):
                    t_op(lambda e, lhs=lhs, jg=jg, hc=hc, tl=tl: e.matmul(
                        pb[tl * 4 + jg][:, :], lhs,
                        owb[:, hc * D + jg * 512:hc * D + (jg + 1) * 512],
                        start=(hc == 0), stop=(hc == 7)),
                        waits=[wlr, ("sV", wOw), wS()]
                        if (hc == 0 and jg == 0) else (),
                        inc="sT" if (hc == 7 and jg == 3) else None)
        wmm = wT()
        for tl in range(2):
            for jg in range(4):
                s_op(lambda e, tl=tl, jg=jg: e.activation(
                    pos[:, tl * D + jg * 512:tl * D + (jg + 1) * 512],
                    pb[tl * 4 + jg][:, :], AF.Copy, scale=1.0),
                    waits=[wmm, wG()], drain=False)
        B.drain("scalar")
        wev = wS()
        for tl in range(2):
            store(cc_po_in[t0 + tl * 128:t0 + (tl + 1) * 128, :],
                  pos[:, tl * D:(tl + 1) * D], waits=[wev])
    g_op(lambda e: e.collective_compute(
        "ReduceScatter", ALU.add, replica_groups=RGP,
        ins=[cc_po_in.tensor.ap().opt()],
        outs=[cc_po_out.tensor.ap().opt()]),
        waits=[wG()], inc="cc")
    cc_rs = B.cur("cc")
    g_op(lambda e: e.wait_ge(sems["cc"], cc_rs))
    dmy2 = store(t1px[0:1, 48:56], ar_sum_out[0:1, :])
    for tt in range(NT):
        wx = load(xa[:, :], x_cur[tt * 128:(tt + 1) * 128, :], "dX",
                  waits=[wV(), wG()])
        wp = load(b1[:, :], cc_po_out[tt * 128:(tt + 1) * 128, :], "dB",
                  waits=[("dG", dmy2), wV()])
        ln_stats(b1[:, :], 0, EPS_LN, waits=[wp])
        ln_apply(b2[:, :], b1[:, :], 0)
        v_op(lambda e: e.tensor_tensor(xa[:, :], xa[:, :], b2[:, :], ALU.add),
             waits=[wx])
        store(x_cur[tt * 128:(tt + 1) * 128, :], xa[:, :], waits=[wV()])

    if stop_after == 3:
        finish_early()
        return _emit(nc, B, sems)
    A = PHASE0
    x8T = salloc("x8T", [128, ND * T], F8, A); A += ND * T
    h8 = salloc("h8", [128, NH * T], F8, A); A += NH * T
    wf4 = salloc("wf4", [128, 2 * ND * 128], F32, A); A += 2 * ND * 128 * 4
    w84 = salloc("w84", [128, 2 * ND * 128], F8, A); A += 2 * ND * 128
    hsp = salloc("hsp", [128, 2 * T], F32, A); A += 2 * T * 4
    wpf = salloc("wpf", [128, 4 * 512], F32, A); A += 4 * 512 * 4
    wp8 = salloc("wp8", [128, 4 * 512], F8, A); A += 4 * 512
    ob4 = salloc("ob4", [128, NT * D], BF16, A); A += NT * D * 2
    x8m = salloc("x8m", [128, D], F8, A); A += D

    v_op(lambda e: e.memset(tiny[:, 3:4], 0.0))
    for tt in range(NT):
        c0 = 32 + tt * 2
        wx = load(xa[:, :], x_cur[tt * 128:(tt + 1) * 128, :], "dX",
                  waits=[wV(), wG()])
        ln_stats(xa[:, :], c0, EPS_LN, waits=[wx])
        ln_apply(b1[:, :], xa[:, :], c0)
        v_op(lambda e: e.reduce_max(tiny[:, 0:1], b1[:, :], axis=AX,
                                    apply_absolute_value=True))
        v_op(lambda e: e.tensor_tensor(tiny[:, 3:4], tiny[:, 3:4],
                                       tiny[:, 0:1], ALU.max))
    xpart(tiny[:, 3:4], 0, "max", waits=[wV()])
    v_op(lambda e: e.memset(t1px[0:1, 56:64], 0.0))
    v_op(lambda e: e.tensor_copy(t1px[0:1, 56:57], t1px[0:1, 0:1]))
    stx = store(ar_xmax_in[0:1, :], t1px[0:1, 56:64], waits=[wV()])
    g_op(lambda e: e.collective_compute(
        "AllReduce", ALU.max, replica_groups=RG8,
        ins=[ar_xmax_in.tensor.ap().opt()],
        outs=[ar_xmax_out.tensor.ap().opt()]),
        waits=[("dG", stx)], inc="cc")
    cc_x = B.cur("cc")
    g_op(lambda e: e.wait_ge(sems["cc"], cc_x))
    lx = store(t1px[0:1, 32:40], ar_xmax_out[0:1, :])
    v_op(lambda e: e.tensor_scalar(t1px[0:1, 8:9], t1px[0:1, 32:33],
                                   1.0 / FP8_MAX, 1e-6, ALU.mult, ALU.max),
         waits=[("dG", lx)])
    v_op(lambda e: e.reciprocal(t1px[0:1, 9:10], t1px[0:1, 8:9]))
    v_op(lambda e: e.tensor_scalar(t1px[0:1, 9:10], t1px[0:1, 9:10], 0.5,
                                   None, ALU.mult))
    v_op(lambda e: e.tensor_tensor(t1px[0:1, 10:11], t1px[0:1, 8:9],
                                   scal[0:1, SFCW:SFCW + 1], ALU.mult))
    v_op(lambda e: e.tensor_scalar(t1px[0:1, 10:11], t1px[0:1, 10:11], 4.0,
                                   None, ALU.mult))
    bcast_row(scal[:, 8:10], t1px[0:1, 9:11], 2, pb[6], waits=[wV()])
    for tt in range(NT):
        c0 = 32 + tt * 2
        wx = load(xa[:, :], x_cur[tt * 128:(tt + 1) * 128, :], "dX",
                  waits=[wV(), wG()])
        ln_apply(b1[:, :], xa[:, :], c0, waits=[wx, wS()])
        s_op(lambda e: e.activation(x8m[:, :], b1[:, :], AF.Copy,
                                    scale=scal[:, RSX:RSX + 1]),
             waits=[wV()])
        wq8 = wS()
        for dc in range(ND):
            transpose(x8T[:, dc * T + tt * 128:dc * T + (tt + 1) * 128],
                      x8m[:, dc * 128:(dc + 1) * 128], eye8[:, :],
                      pb[0][:, 0:128], waits=[wq8] if dc == 0 else [wS()])
    wQ8 = wS()
    v_op(lambda e: e.memset(tiny[:, 3:4], 0.0))
    fc_done = {}
    for hc in range(NH):
        slot = hc % 2
        so = slot * ND * 128
        wl_w = [wQ8] if hc < 2 else [("sT", fc_done[hc - 2])]
        src3 = w_fc_t[:, hc * 128:(hc + 1) * 128].rearrange(
            "(dc p) f -> p dc f", p=128)
        dst3 = wf4[:, so:so + ND * 128].rearrange("p (dc f) -> p dc f", dc=ND)
        usem = "dU%d" % slot
        load(dst3[:, 0:8, :], src3[:, 0:8, :], usem, waits=wl_w)
        wl2 = load(dst3[:, 8:16, :], src3[:, 8:16, :], usem)
        s_op(lambda e, so=so: e.activation(
            w84[:, so:so + ND * 128], wf4[:, so:so + ND * 128], AF.Copy,
            scale=scal[:, RSFCW:RSFCW + 1]), waits=[wl2])
        wq = wS()
        for tg in range(2):
            for dc in range(ND):
                t_op(lambda e, so=so, tg=tg, dc=dc: e.matmul(
                    pb[tg][:, :], w84[:, so + dc * 128:so + (dc + 1) * 128],
                    x8T[:, dc * T + tg * 512:dc * T + (tg + 1) * 512],
                    start=(dc == 0), stop=(dc == ND - 1)),
                    waits=[wq, wV()] if (tg == 0 and dc == 0) else (),
                    inc="sT" if dc == ND - 1 else None)
        fc_done[hc] = B.cur("sT")
        wmm = wT()
        so2 = slot * T
        s_op(lambda e, so2=so2: e.activation(
            hsp[:, so2:so2 + 512], pb[0][:, :], AF.Relu,
            scale=scal[:, SCFC:SCFC + 1]), waits=[wmm, wG()], drain=False)
        s_op(lambda e, so2=so2: e.activation(
            hsp[:, so2 + 512:so2 + 1024], pb[1][:, :], AF.Relu,
            scale=scal[:, SCFC:SCFC + 1]), waits=(), drain=True)
        v_op(lambda e, so2=so2: e.tensor_tensor(
            hsp[:, so2:so2 + T], hsp[:, so2:so2 + T], hsp[:, so2:so2 + T],
            ALU.mult), waits=[wS()])
        v_op(lambda e, so2=so2: e.reduce_max(tiny[:, 0:1],
                                             hsp[:, so2:so2 + T], axis=AX))
        v_op(lambda e: e.tensor_tensor(tiny[:, 3:4], tiny[:, 3:4],
                                       tiny[:, 0:1], ALU.max))
        store(h_spill[hc, :, :], hsp[:, so2:so2 + T], waits=[wV()])
    xpart(tiny[:, 3:4], 0, "max", waits=[wV()])
    v_op(lambda e: e.memset(t1px[0:1, 56:64], 0.0))
    v_op(lambda e: e.tensor_copy(t1px[0:1, 56:57], t1px[0:1, 0:1]))
    sth = store(ar_hmax_in[0:1, :], t1px[0:1, 56:64], waits=[wV()])
    g_op(lambda e: e.collective_compute(
        "AllReduce", ALU.max, replica_groups=RG8,
        ins=[ar_hmax_in.tensor.ap().opt()],
        outs=[ar_hmax_out.tensor.ap().opt()]),
        waits=[("dG", sth)], inc="cc")
    cc_h = B.cur("cc")
    g_op(lambda e: e.wait_ge(sems["cc"], cc_h))
    lh = store(t1px[0:1, 40:48], ar_hmax_out[0:1, :])
    v_op(lambda e: e.tensor_scalar(t1px[0:1, 8:9], t1px[0:1, 40:41],
                                   1.0 / FP8_MAX, 1e-6, ALU.mult, ALU.max),
         waits=[("dG", lh)])
    v_op(lambda e: e.reciprocal(t1px[0:1, 9:10], t1px[0:1, 8:9]))
    v_op(lambda e: e.tensor_scalar(t1px[0:1, 9:10], t1px[0:1, 9:10], 0.5,
                                   None, ALU.mult))
    v_op(lambda e: e.tensor_tensor(t1px[0:1, 10:11], t1px[0:1, 8:9],
                                   scal[0:1, SPJW:SPJW + 1], ALU.mult))
    v_op(lambda e: e.tensor_scalar(t1px[0:1, 10:11], t1px[0:1, 10:11], 4.0,
                                   None, ALU.mult))
    bcast_row(scal[:, 10:12], t1px[0:1, 9:11], 2, pb[6], waits=[wV()])
    wHs = wG()
    for hc in range(NH):
        wl = load(b1[:, 0:T], h_spill[hc, :, :], "dB", waits=[wHs, wS()])
        s_op(lambda e, hc=hc: e.activation(
            h8[:, hc * T:(hc + 1) * T], b1[:, 0:T], AF.Copy,
            scale=scal[:, RSH:RSH + 1]), waits=[wl], drain=False)
    wH8 = wS()
    pj_done = {}
    for jg in range(4):
        for hc in range(NH):
            slot = hc % 4
            so = slot * 512
            idx = jg * NH + hc
            wl_w = [wH8] if idx < 4 else [("sT", pj_done[idx - 4])]
            wl = load(wpf[:, so:so + 512],
                      w_pj_t[hc * 128:(hc + 1) * 128,
                             jg * 512:(jg + 1) * 512], "dD%d" % slot,
                      waits=wl_w)
            s_op(lambda e, so=so: e.activation(
                wp8[:, so:so + 512], wpf[:, so:so + 512], AF.Copy,
                scale=scal[:, RSPJW:RSPJW + 1]), waits=[wl],
                drain=False)
            wq = wS()
            for tt in range(NT):
                t_op(lambda e, so=so, tt=tt, hc=hc: e.matmul(
                    pb[tt][:, :],
                    h8[:, hc * T + tt * 128:hc * T + (tt + 1) * 128],
                    wp8[:, so:so + 512], start=(hc == 0), stop=(hc == NH - 1)),
                    waits=[wq] if tt == 0 else (),
                    inc="sT" if tt == 7 else None)
            pj_done[idx] = B.cur("sT")
        wmm = wT()
        for tt in range(NT):
            s_op(lambda e, tt=tt, jg=jg: e.activation(
                ob4[:, tt * D + jg * 512:tt * D + (jg + 1) * 512],
                pb[tt][:, :], AF.Copy, scale=scal[:, SCPJ:SCPJ + 1]),
                waits=[wmm], drain=False)
    B.drain("scalar")
    wEv4 = wS()
    for tt in range(NT):
        wx = load(xa[:, :], x_cur[tt * 128:(tt + 1) * 128, :], "dX",
                  waits=[wV(), wG()])
        ln_stats(ob4[:, tt * D:(tt + 1) * D], 0, EPS_LN,
                 waits=[wEv4, wx])
        ln_apply(b2[:, :], ob4[:, tt * D:(tt + 1) * D], 0)
        v_op(lambda e: e.tensor_tensor(xa[:, :], xa[:, :], b2[:, :], ALU.add))
        store(y_out[tt * 128:(tt + 1) * 128, :], xa[:, :], waits=[wV()])
    gfin = B.cur("dG")
    g_op(lambda e, g=gfin: e.wait_ge(sems["dG"], g))

    return _emit(nc, B, sems)


def _emit(nc, B, sems):
    with nc.Block() as block:
        def emit_for(name):
            def run(eng):
                hw = {}
                for fn, waits, inc, amt in B.ops[name]:
                    for s, tval in waits:
                        if tval > 0 and hw.get(s, -1) < tval:
                            eng.wait_ge(sems[s], tval)
                            hw[s] = tval
                    instr = fn(eng)
                    if inc is not None:
                        instr.then_inc(sems[inc], amt)
            return run

        block.sync(emit_for("sync"))
        block.tensor(emit_for("tensor"))
        block.vector(emit_for("vector"))
        block.scalar(emit_for("scalar"))
        block.gpsimd(emit_for("gpsimd"))

    return nc


_NC_CACHE = [None]


def kernel(**inputs):
    f32 = np.float32
    x = np.ascontiguousarray(np.asarray(inputs["L0_attn_x"],
                                        dtype=f32).reshape(8192, D))
    Wkv = np.asarray(inputs["L0_attn_Wkv"], dtype=f32)
    bkv = np.asarray(inputs["L0_attn_bkv"], dtype=f32)
    wv_t = np.ascontiguousarray(Wkv[128:256].T)
    bv_rep = np.ascontiguousarray(np.broadcast_to(bkv[128:256][None, :],
                                                  (128, 128)))
    nw = np.ascontiguousarray(np.asarray(inputs["L0_ffn_norm_weight"],
                                         dtype=f32)[None, :])
    gm = np.ascontiguousarray(np.asarray(inputs["L0_ffn_gamma"],
                                         dtype=f32)[None, :])
    w_up_t = np.ascontiguousarray(np.asarray(inputs["L0_ffn_weight_up"],
                                             dtype=f32).T)
    w_dn_t = np.ascontiguousarray(np.asarray(inputs["L0_ffn_weight_down"],
                                             dtype=f32).T)
    qwei = np.asarray(inputs["L1_attn_q_wei"], dtype=f32)
    kwei = np.asarray(inputs["L1_attn_k_wei"], dtype=f32)
    vwei = np.asarray(inputs["L1_attn_v_wei"], dtype=f32)
    owei = np.asarray(inputs["L1_attn_out_wei"], dtype=f32)
    w_fc_t = np.ascontiguousarray(np.asarray(inputs["L1_ffn_w_fc"],
                                             dtype=f32).T)
    w_pj_t = np.ascontiguousarray(np.asarray(inputs["L1_ffn_w_proj"],
                                             dtype=f32).T)

    import ml_dtypes
    eye_bf = np.eye(128, dtype=ml_dtypes.bfloat16)
    eye_f8 = np.eye(128, dtype=ml_dtypes.float8_e4m3fn)
    eye_f32 = np.eye(128, dtype=f32)
    ones_row = np.ones((1, 128), dtype=f32)
    causal = np.ascontiguousarray(
        np.triu(np.full((128, 128), -66504.0, dtype=f32), k=1))

    import os
    sa = int(os.environ.get("KSTOP", "99"))
    if _NC_CACHE[0] is None:
        _NC_CACHE[0] = build_nc(sa)
    nc = _NC_CACHE[0]

    in_maps = []
    for c in range(8):
        hsl = slice((c % 2) * 1024, (c % 2) * 1024 + 1024)
        in_maps.append({
            "x_in": np.ascontiguousarray(x[c * 1024:(c + 1) * 1024]),
            "wv_t": wv_t, "bv_rep": bv_rep, "nw_in": nw, "gm_in": gm,
            "w_up_t": w_up_t, "w_dn_t": w_dn_t,
            "qw": np.ascontiguousarray(qwei[:, hsl]),
            "kw": np.ascontiguousarray(kwei[:, hsl]),
            "vw": np.ascontiguousarray(vwei[:, hsl]),
            "ow": np.ascontiguousarray(owei[hsl, :]),
            "w_fc_t": w_fc_t, "w_pj_t": w_pj_t,
            "st_up": np.ascontiguousarray(
                w_up_t.reshape(-1)[c * 2097152:(c + 1) * 2097152]
                .reshape(256, HID)),
            "st_dn": np.ascontiguousarray(
                w_dn_t.reshape(-1)[c * 2097152:(c + 1) * 2097152]
                .reshape(1024, D)),
            "st_fc": np.ascontiguousarray(
                w_fc_t.reshape(-1)[c * 2097152:(c + 1) * 2097152]
                .reshape(256, HID)),
            "st_pj": np.ascontiguousarray(
                w_pj_t.reshape(-1)[c * 2097152:(c + 1) * 2097152]
                .reshape(1024, D)),
            "eye_bf": eye_bf, "eye_f8": eye_f8, "eye_f32": eye_f32,
            "ones_row": ones_row, "causal": causal,
        })
    res = run_bass_kernel_spmd(nc, in_maps, core_ids=list(range(8)))
    out = np.concatenate([np.asarray(r["y_out"]) for r in res.results], axis=0)
    return out.reshape(4, 2048, 2048)
